# version 08
# baseline (speedup 1.0000x reference)
import numpy as np
import concourse.bass as bass
import concourse.mybir as mybir
from concourse.bass_utils import run_bass_kernel_spmd

F32 = mybir.dt.float32
BF16 = mybir.dt.bfloat16
I32 = mybir.dt.int32
AF = mybir.ActivationFunctionType
ALU = mybir.AluOpType

B = 2
S = 4096
CM = 4096
DH = 1024
T = 1024
CC = 32
DC = 8
KT = 32
HID = 11008
HCN = 86
SHIFT = 12.0
EPS = 1e-5
INV_CM = 1.0 / 4096.0

_DEBUG_DUMPS = ()




class _Sem:
    def __init__(self, nc, name):
        self.h = nc.alloc_semaphore(name)
        self.n = 0


class _Em:
    ENGS = ("tensor", "scalar", "vector", "gpsimd", "sync")

    def __init__(self, nc):
        self.nc = nc
        self.q = {e: [] for e in self.ENGS}

    def sem(self, name):
        return _Sem(self.nc, name)

    def emit(self, eng, fn, waits=(), inc=None, k=1):
        w = {}
        for t in waits:
            if t is None:
                continue
            s, v = t
            if v > w.get(id(s), (s, -1))[1]:
                w[id(s)] = (s, v)
        self.q[eng].append((list(w.values()), fn, inc, k))
        if inc is not None:
            inc.n += k
            return (inc, inc.n)
        return None

    def build(self, block):
        def mk(name):
            def body(eng):
                for waits, fn, inc, k in self.q[name]:
                    for (s, v) in waits:
                        eng.wait_ge(s.h, v)
                    ins = fn(eng)
                    if inc is not None:
                        ins.then_inc(inc.h, k)
            return body
        block.tensor(mk("tensor"))
        block.scalar(mk("scalar"))
        block.vector(mk("vector"))
        block.gpsimd(mk("gpsimd"))
        block.sync(mk("sync"))


def build(dumps=(), upto="FIN"):
    nc = bass.Bass("TRN2", target_bir_lowering=False)
    PH = ["A", "B", "C", "D1", "D2", "FIN"]
    lvl = PH.index(upto)

    xt_full = nc.dram_tensor("xt_full", [CM, S], F32, kind="ExternalInput").ap()
    xt_own = nc.dram_tensor("xt_own", [CM, T], F32, kind="ExternalInput").ap()
    wq_r = nc.dram_tensor("wq_r", [16, 128, CC, 128], F32, kind="ExternalInput").ap()
    wv_r = nc.dram_tensor("wv_r", [CC, 128, DH], F32, kind="ExternalInput").ap()
    bqk_col = nc.dram_tensor("bqk_col", [128, 16], F32, kind="ExternalInput").ap()
    bv_row = nc.dram_tensor("bv_row", [1, DH], F32, kind="ExternalInput").ap()
    wo_r = nc.dram_tensor("wo_r", [32, 128, DC, 128], F32, kind="ExternalInput").ap()
    bo_col = nc.dram_tensor("bo_col", [128, 32], F32, kind="ExternalInput").ap()
    wg_r = nc.dram_tensor("wg_r", [HCN, 128, CC, 128], I32, kind="ExternalInput").ap()
    wu_r = nc.dram_tensor("wu_r", [HCN, 128, CC, 128], I32, kind="ExternalInput").ap()
    wd_r = nc.dram_tensor("wd_r", [4, HCN, 128, 8, 128], I32, kind="ExternalInput").ap()
    sg_in = nc.dram_tensor("sg_in", [128, 1], F32, kind="ExternalInput").ap()
    su_in = nc.dram_tensor("su_in", [128, 1], F32, kind="ExternalInput").ap()
    sd_in = nc.dram_tensor("sd_in", [128, 1], F32, kind="ExternalInput").ap()

    out_t = nc.dram_tensor("out", [CM, T], F32, kind="ExternalOutput").ap()

    V_d = nc.dram_tensor("V_d", [KT, 128, DH], BF16, kind="Internal").ap()
    mid_d = nc.dram_tensor("mid_d", [HCN, 128, T], BF16, kind="Internal").ap()

    cdb = nc.alloc_sbuf_tensor("cdb", [128, 8], F32)
    for i, v in enumerate([0.03125, -SHIFT, EPS, INV_CM, 1.0, -0.5, -1.0]):
        nc.const_aps.register(F32, v, cdb[:, i:i + 1])
    ones_bf = nc.alloc_sbuf_tensor("ones_bf", [1, 512], BF16)
    ones_col = nc.alloc_sbuf_tensor("ones_col", [128, 1], BF16)
    ones_f32 = nc.alloc_sbuf_tensor("ones_f32", [1, 128], F32)
    sg_sb = nc.alloc_sbuf_tensor("sg_sb", [128, 1], F32)
    su_sb = nc.alloc_sbuf_tensor("su_sb", [128, 1], F32)
    sd_sb = nc.alloc_sbuf_tensor("sd_sb", [128, 1], F32)
    bqk_sb = nc.alloc_sbuf_tensor("bqk_sb", [128, 16], F32)
    bo_sb2 = nc.alloc_sbuf_tensor("bo_sb2", [128, 32], F32)

    def at(name, shape, dtype, off):
        end = off + int(np.prod(shape[1:])) * mybir.dt.size(dtype)
        assert off >= 28672 and end <= 229280, (name, off, end)
        return nc.alloc_sbuf_tensor_at(name, shape, dtype, offset=off)

    rowsA = at("rowsA", [1, 4096], F32, 28672)
    rowsB = at("rowsB", [1, 1024], F32, 45056)
    brows = at("brows", [1, 1024], BF16, 49152)
    RA = 63488
    RB = RA + 65536
    RC = RB + 65536
    RD = RC + 16384

    xq_bf = at("xq_bf", [128, CC * T], BF16, RA)
    PT = at("PT", [128, KT * T], BF16, RA)
    y1 = at("y1", [128, 32 * T], BF16, RA)
    wgbuf = at("wgbuf", [128, 2 * CC * 128], BF16, RA)
    wubuf = at("wubuf", [128, 2 * CC * 128], BF16, RA + 16384)
    gtbuf = at("gtbuf", [128, 2 * T], BF16, RD)
    utbuf = at("utbuf", [128, 2 * T], BF16, RD + 4096)
    acc = at("acc", [128, 8 * T], F32, RA)
    wdgbuf = at("wdgbuf", [128, 2 * 6 * 1024], BF16, RA + 32768)
    K_sb = at("K_sb", [128, DC * S], BF16, RB)
    x1 = at("x1", [128, 32 * T], BF16, RB)
    V_stage = at("V_stage", [128, 8 * DH], BF16, RC)
    Q_sb = at("Q_sb", [128, DC * T], BF16, RC)
    O_sb = at("O_sb", [128, DC * T], BF16, RC)
    mb_sb = at("mb_sb", [128, T], BF16, RC)
    rsb_sb = at("rsb_sb", [128, T], BF16, RC + 2048)
    t0buf = at("t0buf", [128, T], BF16, RC + 4096)
    t0gbuf = at("t0gbuf", [128, T], BF16, RC + 6144)
    midbufs = at("midbufs", [128, 4 * T], BF16, RD + 8192)
    midgbuf = at("midgbuf", [128, 2 * 6 * 1024], BF16, RC)
    m2b_sb = at("m2b_sb", [128, T], BF16, RC + 4096)
    r2b_sb = at("r2b_sb", [128, T], F32, RC + 8192)
    t02buf = at("t02buf", [128, T], BF16, RC + 12288)
    t02gbuf = at("t02gbuf", [128, T], BF16, RC + 14336)
    kslab = at("kslab", [128, 2 * 2048], BF16, RD)
    wvbuf = at("wvbuf", [128, 2 * 4 * 512], BF16, RD + 8192)
    vbuf = at("vbuf", [128, 2 * 2 * DH], BF16, RD)
    rb_sb = at("rb_sb", [128, T], F32, RD + 8192)
    wobuf = at("wobuf", [128, 2 * DH], BF16, RD)
    xcbuf = at("xcbuf", [128, T], F32, RD + 4096)
    sqbuf = at("sqbuf", [128, 2 * T], BF16, RD + 8192)
    y2buf = at("y2buf", [128, 4 * T], F32, RD)

    pp = [nc.alloc_psum_tensor(f"pp{i}", [128, 1024], F32) for i in range(4)]

    em = _Em(nc)
    s_pe = em.sem("pe")
    s_act = em.sem("act")
    s_dve = em.sem("dve")
    s_gp = em.sem("gp")

    def chan(name):
        return em.sem(name)

    def dma(q, ch, dst, src, waits=()):
        return em.emit(q, lambda e, d=dst, s=src: e.dma_start(d, s), waits, ch, 16)

    ch_binit = chan("binit")
    ch_sinit = chan("sinit")
    ch_xq = [chan(f"xq{i}") for i in range(8)]
    ch_ksl = [chan(f"ksl{i}") for i in range(2)]
    ch_wv = [chan(f"wv{i}") for i in range(2)]
    ch_vb = [chan(f"vb{i}") for i in range(2)]
    ch_wo = [chan(f"wo{i}") for i in range(2)]
    ch_xc = chan("xc")
    ch_wg = [chan(f"wg{i}") for i in range(2)]
    ch_wu = [chan(f"wu{i}") for i in range(2)]
    ch_wdg = [chan(f"wdg{i}") for i in range(2)]
    ch_mg = [chan(f"mg{i}") for i in range(2)]
    ch_vst = chan("vst")
    ch_mid = [chan(f"mst{i}") for i in range(4)]
    ch_out = [chan(f"outst{i}") for i in range(2)]

    def mm(o, l, r, st, sp, waits=(), inc=None):
        return em.emit("tensor",
                       lambda e, o=o, l=l, r=r, st=st, sp=sp:
                       e.matmul(o, l, r, start=st, stop=sp),
                       waits, s_pe if inc else None, 1)

    xq3 = xq_bf.ap().rearrange("p (cc t) -> p cc t", cc=CC)
    K3 = K_sb.ap().rearrange("p (dc t) -> p dc t", dc=DC)
    Q3 = Q_sb.ap().rearrange("p (dc t) -> p dc t", dc=DC)
    PT3 = PT.ap().rearrange("p (kt t) -> p kt t", kt=KT)
    O3 = O_sb.ap().rearrange("p (dc t) -> p dc t", dc=DC)
    y13 = y1.ap().rearrange("p (hc t) -> p hc t", hc=32)
    x13 = x1.ap().rearrange("p (cc t) -> p cc t", cc=32)
    acc3 = acc.ap().rearrange("p (dc t) -> p dc t", dc=8)
    mid4 = midbufs.ap().rearrange("p (i t) -> p i t", i=4)
    xt_own3 = xt_own.rearrange("(hc p) t -> hc p t", p=128)
    m_row = rowsA[0:1, 0:1024]
    v_row = rowsA[0:1, 1024:2048]
    r_row = rowsA[0:1, 2048:3072]
    tmp_row = rowsA[0:1, 3072:4096]

    def init_fn(e):
        e.memset(cdb[:, 0:1], 0.03125)
        e.memset(cdb[:, 1:2], -SHIFT)
        e.memset(cdb[:, 2:3], EPS)
        e.memset(cdb[:, 3:4], INV_CM)
        e.memset(cdb[:, 4:5], 1.0)
        e.memset(ones_bf[:], 1.0)
        e.memset(ones_col[:], 1.0)
        return e.memset(ones_f32[:], 1.0)
    tok_init = em.emit("vector", init_fn, (), s_dve, 1)
    tok_binit = dma("gpsimd", ch_binit, brows[0:1, 0:1024], bv_row)
    dma("sync", ch_sinit, sg_sb[:], sg_in)
    dma("sync", ch_sinit, su_sb[:], su_in)
    dma("sync", ch_sinit, sd_sb[:], sd_in)
    dma("sync", ch_sinit, bqk_sb[:], bqk_col)
    tok_sinit = dma("sync", ch_sinit, bo_sb2[:], bo_col)
    tok_bv = tok_binit
    tok_bq = tok_bk = tok_bo = tok_sg = tok_su = tok_sd = tok_sinit

    pp_gate = [tok_init] * 4

    xq_sub_gate = [None] * 8
    kslab_gate = [None, None]
    wv_gate = [None, None]
    vstage_gate = None

    def src_of(q):
        if q < 4:
            return xt_full.rearrange("(cc p) t -> p cc t", p=128)[:, :, q * T:(q + 1) * T]
        return xt_own.rearrange("(cc p) t -> p cc t", p=128)

    def load_sub(src3, sc):
        return dma("gpsimd", ch_xq[sc], xq3[:, sc * 4:(sc + 1) * 4, :],
                   src3[:, sc * 4:(sc + 1) * 4, :], [xq_sub_gate[sc]])

    def qk_section(toks_xq, slab0, bcol0, tok_b, dst3, evac0_waits=(), pre0=None):
        last = None
        for i in range(DC):
            sl = [(2 * i + h) % 2 for h in range(2)]
            if i == 0 and pre0 is not None:
                tok_sl = pre0
            else:
                tok_sl = [
                    dma("gpsimd", ch_ksl[sl[h]], kslab[:, sl[h] * 2048:(sl[h] + 1) * 2048],
                        wq_r[slab0 + i][:, h * 16:(h + 1) * 16, :].rearrange("p a b -> p (a b)"),
                        [kslab_gate[sl[h]]])
                    for h in range(2)
                ]
            ppi = i % 2
            ppt = pp[ppi]
            tok_pe = None
            for cc in range(CC):
                w = []
                if cc == 0:
                    w = [tok_sl[0], pp_gate[ppi], tok_init, tok_b]
                elif cc == 16:
                    w = [tok_sl[1]]
                if cc % 4 == 0:
                    w = w + [toks_xq[cc // 4]]
                lt = kslab[:, sl[cc // 16] * 2048 + (cc % 16) * 128:
                           sl[cc // 16] * 2048 + (cc % 16 + 1) * 128]
                for h in range(2):
                    tok = mm(ppt[:, h * 512:(h + 1) * 512], lt,
                             xq3[:, cc, h * 512:(h + 1) * 512], cc == 0, cc == CC - 1, w,
                             inc=(cc in (15, CC - 1) and h == 1))
                    w = []
                if cc == 15:
                    kslab_gate[sl[0]] = tok
                elif cc == CC - 1:
                    tok_pe = tok
            kslab_gate[sl[1]] = tok_pe
            tok_e = em.emit("scalar",
                            lambda e, d=dst3[:, i, :], s=ppt[:],
                                   b=bqk_sb[:, bcol0 + i: bcol0 + i + 1]:
                            e.activation(d, s, AF.Identity, bias=b),
                            [tok_pe] + (list(evac0_waits) if i == 0 else []), s_act, 1)
            pp_gate[ppi] = tok_e
            last = tok_pe
        return last

    src = src_of(0)
    toks_xq = [None] * 8
    toks_xq[0] = load_sub(src, 0)
    pre0 = [dma("gpsimd", ch_ksl[h], kslab[:, h * 2048:(h + 1) * 2048],
                wq_r[8][:, h * 16:(h + 1) * 16, :].rearrange("p a b -> p (a b)"))
            for h in range(2)]
    for sc in range(1, 8):
        toks_xq[sc] = load_sub(src, sc)
    for q in range(4):
        K_dst = K_sb.ap().rearrange("p (dc t) -> p dc t", dc=DC)[:, :, q * T:(q + 1) * T]
        qk_section(toks_xq, 8, 8, tok_bk, K_dst, pre0=pre0)
        next_src = src_of(q + 1)
        next_toks = [None] * 8
        next_pre0 = [None, None]
        tok_pe_v = None
        toks_ve = []
        wv4 = wvbuf.ap().rearrange("p (s j f) -> p s j f", s=2, j=4)
        for fh in range(2):
            for g in range(8):
                slot = g % 2
                tok_v = dma("gpsimd", ch_wv[slot], wv4[:, slot, :, :],
                            wv_r[4 * g:4 * g + 4, :, fh * 512:(fh + 1) * 512]
                            .rearrange("c p f -> p c f"), [wv_gate[slot]])
                w = [tok_v]
                if g == 0:
                    w = w + [tok_bv] + list(pp_gate)
                if fh == 0:
                    w = w + [toks_xq[g]]
                tok_cc = None
                for j in range(4):
                    cc = 4 * g + j
                    for tc in range(8):
                        tok_cc = mm(pp[tc // 2][:, (tc % 2) * 512:(tc % 2) * 512 + 512],
                                    xq3[:, cc, tc * 128:(tc + 1) * 128],
                                    wv4[:, slot, j, :],
                                    cc == 0, False, w, inc=(j == 3 and tc == 7))
                        w = []
                wv_gate[slot] = tok_cc
                if fh == 1:
                    xq_sub_gate[g] = tok_cc
                    if g >= 1:
                        next_toks[g - 1] = load_sub(next_src, g - 1)
                    if g in (2, 3):
                        nslab = 8 if q < 3 else 0
                        next_pre0[g - 2] = dma(
                            "gpsimd", ch_ksl[g - 2],
                            kslab[:, (g - 2) * 2048:(g - 1) * 2048],
                            wq_r[nslab][:, (g - 2) * 16:(g - 1) * 16, :].rearrange("p a b -> p (a b)"),
                            [kslab_gate[g - 2]])
            for tc in range(8):
                tok_pe_v = mm(pp[tc // 2][:, (tc % 2) * 512:(tc % 2) * 512 + 512],
                              ones_bf[0:1, 0:128],
                              brows[0:1, fh * 512:(fh + 1) * 512],
                              False, True, (), inc=(tc == 7))
            toks_ve = []
            for tc in range(8):
                toks_ve.append(em.emit(
                    "scalar",
                    lambda e, d=V_stage[:, tc * 1024 + fh * 512: tc * 1024 + fh * 512 + 512],
                           s=pp[tc // 2][:, (tc % 2) * 512:(tc % 2) * 512 + 512]:
                    e.copy(d, s),
                    [tok_pe_v] + ([vstage_gate] if (fh == 0 and tc == 0) else []),
                    s_act, 1))
            for i in range(4):
                pp_gate[i] = toks_ve[2 * i + 1]
        tok_vst = dma("sync", ch_vst, V_d[q * 8:(q + 1) * 8].rearrange("tc p f -> p tc f"),
                      V_stage.ap().rearrange("p (tc f) -> p tc f", tc=8),
                      [toks_ve[-1]])
        next_toks[7] = load_sub(next_src, 7)
        toks_xq = next_toks
        pre0 = next_pre0 if next_pre0[0] is not None else None
        vstage_gate = tok_vst
    tok_q_last = qk_section(toks_xq, 0, 0, tok_bq, Q3, evac0_waits=[vstage_gate], pre0=pre0)
    a_done_act = (s_act, s_act.n)

    pt_done = tok_rec = pv_done_dve = pv_pe_done = None
    if lvl >= 1:
        exp_toks = []
        for kt in range(KT):
            ppi = kt % 2
            ppt = pp[ppi]
            w = [pp_gate[ppi]] + ([a_done_act] if kt == 0 else [])
            tok_s = None
            for dc in range(DC):
                for qh in range(2):
                    tok_s = mm(ppt[:, qh * 512:(qh + 1) * 512],
                               K3[:, dc, kt * 128:(kt + 1) * 128],
                               Q3[:, dc, qh * 512:(qh + 1) * 512],
                               dc == 0, dc == DC - 1, w,
                               inc=(dc == DC - 1 and qh == 1))
                    w = []
            tok_e = em.emit("scalar",
                            lambda e, d=PT3[:, kt, :], s=ppt[:]:
                            e.activation(d, s, AF.Exp, bias=-SHIFT, scale=0.03125),
                            [tok_s], s_act, 1)
            pp_gate[ppi] = tok_e
            exp_toks.append(tok_e)
            if kt >= 1:
                for qh in range(2):
                    w = [exp_toks[kt - 1]] + ([pp_gate[2]] if kt == 1 else [])
                    mm(pp[2][0:1, qh * 512:(qh + 1) * 512], ones_col[:, 0:1],
                       PT3[:, kt - 1, qh * 512:(qh + 1) * 512],
                       kt == 1, False, w)
        pt_done = (s_act, s_act.n)
        for qh in range(2):
            tok_sums = mm(pp[2][0:1, qh * 512:(qh + 1) * 512], ones_col[:, 0:1],
                          PT3[:, KT - 1, qh * 512:(qh + 1) * 512],
                          False, True, [exp_toks[KT - 1]], inc=(qh == 1))
        def rec_fn(e):
            e.activation(rowsA[0:1, 0:512], pp[2][0:1, 0:512], AF.Ln)
            e.activation(rowsA[0:1, 512:1024], pp[2][0:1, 512:1024], AF.Ln)
            e.activation(rowsB[0:1, 0:512], rowsA[0:1, 0:512], AF.Exp, scale=-1.0)
            return e.activation(rowsB[0:1, 512:1024], rowsA[0:1, 512:1024], AF.Exp, scale=-1.0)
        tok_rec = em.emit("scalar", rec_fn, [tok_sums], s_act, 1)
        pp_gate[2] = tok_rec
        tok_bc = None
        for qh in range(2):
            tok_bc = mm(pp[3][:, qh * 512:(qh + 1) * 512], ones_f32[0:1, :],
                        rowsB[0:1, qh * 512:(qh + 1) * 512], True, True,
                        [tok_rec, pp_gate[3]] if qh == 0 else [], inc=(qh == 1))
        tok_rb = em.emit("scalar", lambda e: e.copy(rb_sb[:], pp[3][:]),
                         [tok_bc], s_act, 1)
        pp_gate[3] = tok_rb
        vb_gate = [tok_q_last] * 2
        vb4 = vbuf.ap().rearrange("p (s k f) -> p s k f", s=2, k=2)
        for qh in range(2):
            tok_pe_pv = None
            for kp in range(KT // 2):
                slot = kp % 2
                tok_v = dma("sync", ch_vb[slot], vb4[:, slot, :, :],
                            V_d[2 * kp:2 * kp + 2].rearrange("k p f -> p k f"),
                            [vb_gate[slot]] + ([vstage_gate] if kp == 0 else []))
                w = [tok_v] + (list(pp_gate) if kp == 0 else [])
                for ki in range(2):
                    kc = 2 * kp + ki
                    for dc in range(DC):
                        tok_pe_pv = mm(pp[dc // 2][:, (dc % 2) * 512:(dc % 2) * 512 + 512],
                                       vb4[:, slot, ki, dc * 128:(dc + 1) * 128],
                                       PT3[:, kc, qh * 512:(qh + 1) * 512],
                                       kc == 0, kc == KT - 1, w,
                                       inc=(ki == 1 and dc == DC - 1))
                        w = []
                vb_gate[slot] = tok_pe_pv
            toks = []
            for dc in range(8):
                toks.append(em.emit(
                    "vector",
                    lambda e, d=O3[:, dc, qh * 512:(qh + 1) * 512],
                           s=pp[dc // 2][:, (dc % 2) * 512:(dc % 2) * 512 + 512],
                           rb=rb_sb[:, qh * 512:(qh + 1) * 512]:
                    e.scalar_tensor_tensor(d, s, 1.0, rb, ALU.mult, ALU.mult),
                    [tok_pe_pv, tok_rb], s_dve, 1))
            for i in range(4):
                pp_gate[i] = toks[2 * i + 1]
        pv_done_dve = (s_dve, s_dve.n)
        pv_pe_done = (s_pe, s_pe.n)

    def ln_rows(tok_in, mb_dst, rb_dst, gates):
        def rows_fn(e):
            e.tensor_scalar_mul(m_row[0:1, 0:512], pp[2][0:1, 0:512], INV_CM)
            e.tensor_scalar_mul(m_row[0:1, 512:1024], pp[2][0:1, 512:1024], INV_CM)
            e.tensor_scalar_mul(v_row[0:1, 0:512], pp[3][0:1, 0:512], INV_CM)
            e.tensor_scalar_mul(v_row[0:1, 512:1024], pp[3][0:1, 512:1024], INV_CM)
            e.tensor_tensor(tmp_row, m_row, m_row, ALU.mult)
            return e.tensor_tensor(v_row, v_row, tmp_row, ALU.subtract)
        tok_rows = em.emit("vector", rows_fn, [tok_in], s_dve, 1)
        def rstd_fn(e):
            e.activation(tmp_row, v_row, AF.Ln, bias=EPS)
            return e.activation(r_row, tmp_row, AF.Exp, scale=-0.5)
        tok_r = em.emit("scalar", rstd_fn, [tok_rows], s_act, 1)
        tok_bc = None
        for i, src_row in enumerate((m_row, r_row)):
            for qh in range(2):
                tok_bc = mm(pp[i][:, qh * 512:(qh + 1) * 512], ones_f32[0:1, :],
                            src_row[0:1, qh * 512:(qh + 1) * 512], True, True,
                            ([tok_r] + gates) if (i == 0 and qh == 0) else [],
                            inc=(i == 1 and qh == 1))
        def cp_fn(e):
            e.copy(mb_dst[:], pp[0][:])
            return e.copy(rb_dst[:], pp[1][:])
        tok_mb = em.emit("scalar", cp_fn, [tok_bc], s_act, 1)
        return tok_mb

    tok_stats = tok_mb = x1_done = None
    if lvl >= 2:
        wo_gate = [pv_done_dve, pv_done_dve]
        xc_gate = [pv_done_dve]
        sq_gate = [None, None]
        for hc in range(32):
            slot = hc % 2
            tok_w = dma("gpsimd", ch_wo[slot], wobuf[:, slot * DH:(slot + 1) * DH],
                        wo_r[hc].rearrange("p a b -> p (a b)"), [wo_gate[slot]])
            tok_x = dma("sync", ch_xc, xcbuf[:], xt_own3[hc], [xc_gate[0]])
            ppi = hc % 2
            ppt = pp[ppi]
            w = [tok_w, pp_gate[ppi], tok_bo] + ([pv_done_dve] if hc == 0 else [])
            tok_y = None
            for dc in range(DC):
                for qh in range(2):
                    tok_y = mm(ppt[:, qh * 512:(qh + 1) * 512],
                               wobuf[:, slot * DH + dc * 128: slot * DH + (dc + 1) * 128],
                               O3[:, dc, qh * 512:(qh + 1) * 512], dc == 0, dc == DC - 1, w,
                               inc=(dc == DC - 1 and qh == 1))
                    w = []
            wo_gate[slot] = tok_y
            tok_add = em.emit("vector",
                              lambda e, d=y13[:, hc, :], s=ppt[:], x=xcbuf[:],
                                     b=bo_sb2[:, hc:hc + 1]:
                              e.scalar_tensor_tensor(d, s, b, x, ALU.add, ALU.add),
                              [tok_y, tok_x], s_dve, 1)
            xc_gate[0] = tok_add
            pp_gate[ppi] = tok_add
            tok_sc = em.emit("scalar",
                             lambda e, hc=hc, slot=slot:
                             e.square(sqbuf[:, slot * T:(slot + 1) * T], y13[:, hc, :]),
                             [tok_add, sq_gate[slot]], s_act, 1)
            w = [tok_add, tok_sc] + ([pp_gate[2], pp_gate[3]] if hc == 0 else [])
            for qh in range(2):
                mm(pp[2][0:1, qh * 512:(qh + 1) * 512], ones_col[:, 0:1],
                   y13[:, hc, qh * 512:(qh + 1) * 512],
                   hc == 0, hc == 31, w)
                w = []
                tok_stats = mm(pp[3][0:1, qh * 512:(qh + 1) * 512], ones_col[:, 0:1],
                               sqbuf[:, slot * T + qh * 512: slot * T + qh * 512 + 512],
                               hc == 0, hc == 31, (), inc=(qh == 1))
            sq_gate[slot] = tok_stats

        tok_mb = ln_rows(tok_stats, mb_sb, rsb_sb, [pp_gate[0], pp_gate[1]])
        for i in range(4):
            pp_gate[i] = tok_mb
        x1_toks = []
        for hc in range(32):
            def ap_fn(e, hc=hc):
                e.tensor_tensor(t0buf[:], y13[:, hc, :], mb_sb[:], ALU.subtract)
                return e.tensor_tensor(x13[:, hc, :], t0buf[:], rsb_sb[:], ALU.mult)
            x1_toks.append(em.emit("vector", ap_fn, [tok_mb] if hc == 0 else [], s_dve, 1))
        x1_done = x1_toks[-1]

    mid_store = [None] * HCN
    d1_last_mid = None
    if lvl >= 3:
        wg_gate = [x1_toks[3], x1_toks[7]]
        wu_gate = [x1_toks[11], x1_toks[15]]
        gt_gate = [None, None]
        mid_gate = [None, None, None, None]
        for hc in range(HCN):
            slot = hc % 2
            tok_g = dma("gpsimd", ch_wg[slot], wgbuf[:, slot * 4096:(slot + 1) * 4096],
                        wg_r[hc].rearrange("p a b -> p (a b)"), [wg_gate[slot]])
            tok_u = dma("gpsimd", ch_wu[slot], wubuf[:, slot * 4096:(slot + 1) * 4096],
                        wu_r[hc].rearrange("p a b -> p (a b)"), [wu_gate[slot]])
            ppg = pp[hc % 2]
            ppu = pp[2 + hc % 2]
            w = [tok_g, pp_gate[hc % 2]] + ([x1_done] if hc == 0 else [])
            tok_gmm = None
            for cc in range(CC):
                for qh in range(2):
                    tok_gmm = mm(ppg[:, qh * 512:(qh + 1) * 512],
                                 wgbuf[:, slot * 4096 + cc * 128: slot * 4096 + (cc + 1) * 128],
                                 x13[:, cc, qh * 512:(qh + 1) * 512],
                                 cc == 0, cc == CC - 1, w,
                                 inc=(cc == CC - 1 and qh == 1))
                    w = []
            w = [tok_u, pp_gate[2 + hc % 2]]
            tok_umm = None
            for cc in range(CC):
                for qh in range(2):
                    tok_umm = mm(ppu[:, qh * 512:(qh + 1) * 512],
                                 wubuf[:, slot * 4096 + cc * 128: slot * 4096 + (cc + 1) * 128],
                                 x13[:, cc, qh * 512:(qh + 1) * 512],
                                 cc == 0, cc == CC - 1, w,
                                 inc=(cc == CC - 1 and qh == 1))
                    w = []
            wg_gate[slot] = tok_gmm
            wu_gate[slot] = tok_umm
            def ev_fn(e, slot=slot, ppg=ppg, ppu=ppu):
                e.activation(gtbuf[:, slot * T:(slot + 1) * T], ppg[:], AF.Silu,
                             scale=sg_sb[:, 0:1])
                return e.activation(utbuf[:, slot * T:(slot + 1) * T], ppu[:],
                                    AF.Identity, scale=su_sb[:, 0:1])
            tok_si = em.emit("scalar", ev_fn,
                             [tok_gmm, tok_umm, gt_gate[slot], tok_sg, tok_su],
                             s_act, 1)
            pp_gate[hc % 2] = tok_si
            pp_gate[2 + hc % 2] = tok_si
            ms = hc % 4
            tok_mid = em.emit("vector",
                              lambda e, d=mid4[:, ms, :],
                                     a=gtbuf[:, slot * T:(slot + 1) * T],
                                     b=utbuf[:, slot * T:(slot + 1) * T]:
                              e.tensor_tensor(d, a, b, ALU.mult),
                              [tok_si, mid_gate[ms]], s_dve, 1)
            gt_gate[slot] = tok_mid
            d1_last_mid = tok_mid
            tok_mst = dma("sync", ch_mid[ms], mid_d[hc], mid4[:, ms, :], [tok_mid])
            mid_gate[ms] = tok_mst
            mid_store[hc] = tok_mst

    if lvl >= 4:
        GDEF = 6
        gstarts = list(range(0, 84, GDEF)) + [84]
        gsizes = [GDEF] * 14 + [2]
        wdg3 = wdgbuf.ap().rearrange("p (s g x) -> p s g x", s=2, g=GDEF)
        midg3 = midgbuf.ap().rearrange("p (s g t) -> p s g t", s=2, g=GDEF)
        wdg_gate = [x1_toks[21], x1_toks[27]]
        midg_gate = [None, None]
        y2b_gate = [None, None]
        tok_stats2 = None
        for qtr in range(4):
            acc_tok = [None] * 8
            for gi, (g0, G) in enumerate(zip(gstarts, gsizes)):
                slot = gi % 2
                tok_wd = dma("gpsimd", ch_wdg[slot], wdg3[:, slot, 0:G, :],
                             wd_r[qtr, g0:g0 + G].rearrange("g p a b -> p g (a b)"),
                             [wdg_gate[slot]])
                tok_md = dma("scalar", ch_mg[slot], midg3[:, slot, 0:G, :],
                             mid_d[g0:g0 + G].rearrange("g p t -> p g t"),
                             [midg_gate[slot]] + [mid_store[h] for h in range(g0, g0 + G)])
                last_pe = None
                for dc in range(8):
                    ppi = dc % 2
                    ppt = pp[ppi]
                    w = [pp_gate[ppi]] + ([tok_wd, tok_md] if dc < 2 else [])
                    tok_d = None
                    for i in range(G):
                        for qh in range(2):
                            tok_d = mm(ppt[:, qh * 512:(qh + 1) * 512],
                                       wdg3[:, slot, i, dc * 128:(dc + 1) * 128],
                                       midg3[:, slot, i, qh * 512:(qh + 1) * 512],
                                       i == 0, i == G - 1, w,
                                       inc=(i == G - 1 and qh == 1))
                            w = []
                    last_pe = tok_d
                    if gi == 0:
                        tok_a = em.emit("vector",
                                        lambda e, d=acc3[:, dc, :], s=ppt[:]:
                                        e.tensor_copy(d, s), [tok_d], s_dve, 1)
                    else:
                        tok_a = em.emit("vector",
                                        lambda e, d=acc3[:, dc, :], s=ppt[:]:
                                        e.tensor_tensor(d, s, d, ALU.add),
                                        [tok_d], s_dve, 1)
                    pp_gate[ppi] = tok_a
                    acc_tok[dc] = tok_a
                wdg_gate[slot] = last_pe
                midg_gate[slot] = last_pe
            for dc in range(8):
                ch = qtr * 8 + dc
                slot = dc % 2
                tok_y2 = em.emit("vector",
                                 lambda e, d=x13[:, ch, :],
                                        a=acc3[:, dc, :], x=x13[:, ch, :]:
                                 e.scalar_tensor_tensor(d, a, sd_sb[:, 0:1], x,
                                                        ALU.mult, ALU.add),
                                 [acc_tok[dc], tok_sd], s_dve, 1)
                tok_sc2 = em.emit("scalar",
                                  lambda e, slot=slot, ch=ch:
                                  e.square(sqbuf[:, slot * T:(slot + 1) * T], x13[:, ch, :]),
                                  [tok_y2, y2b_gate[slot]] +
                                  ([tok_stats, sq_gate[0], sq_gate[1]] if ch == 0 else []),
                                  s_act, 1)
                w = [tok_y2, tok_sc2] + ([pp_gate[2], pp_gate[3]] if ch == 0 else [])
                for qh in range(2):
                    mm(pp[2][0:1, qh * 512:(qh + 1) * 512], ones_col[:, 0:1],
                       x13[:, ch, qh * 512:(qh + 1) * 512],
                       ch == 0, ch == 31, w)
                    w = []
                    tok_stats2 = mm(pp[3][0:1, qh * 512:(qh + 1) * 512], ones_col[:, 0:1],
                                    sqbuf[:, slot * T + qh * 512: slot * T + qh * 512 + 512],
                                    ch == 0, ch == 31, (), inc=(qh == 1))
                y2b_gate[slot] = tok_stats2

        tok_m2b = ln_rows(tok_stats2, m2b_sb, r2b_sb, [pp_gate[0], pp_gate[1]])
        ob_gate = [None, None, None, None]
        ch_out = ch_out + [chan("outst2"), chan("outst3")]
        out3 = out_t.rearrange("(hc p) t -> hc p t", p=128)
        for ch in range(32):
            slot = ch % 4
            eng = "gpsimd" if ch % 6 == 5 else "vector"
            t02 = t02gbuf if eng == "gpsimd" else t02buf
            def ap2_fn(e, slot=slot, ch=ch, t02=t02):
                e.tensor_tensor(t02[:], x13[:, ch, :], m2b_sb[:], ALU.subtract)
                return e.tensor_tensor(y2buf[:, slot * T:(slot + 1) * T], t02[:],
                                       r2b_sb[:], ALU.mult)
            tok_o = em.emit(eng, ap2_fn, [ob_gate[slot], tok_m2b],
                            s_gp if eng == "gpsimd" else s_dve, 1)
            tok_out = dma("sync", ch_out[slot], out3[ch],
                          y2buf[:, slot * T:(slot + 1) * T], [tok_o])
            ob_gate[slot] = tok_out

    for name in dumps:
        if name == "qkv":
            dq = nc.dram_tensor("d_q", [128, DC * T], BF16, kind="ExternalOutput").ap()
            dk = nc.dram_tensor("d_k", [128, DC * S], BF16, kind="ExternalOutput").ap()
            dv = nc.dram_tensor("d_v", [KT, 128, DH], BF16, kind="ExternalOutput").ap()
            dma("sync", ch_out[0], dq, Q_sb.ap(), [a_done_act])
            dma("sync", ch_out[0], dk, K_sb.ap(), [a_done_act])
            dma("sync", ch_out[0], dv, V_d[:], [vstage_gate])
        if name == "pt" and lvl >= 1:
            dp = nc.dram_tensor("d_pt", [128, KT * T], BF16, kind="ExternalOutput").ap()
            dr = nc.dram_tensor("d_rec", [1, 1024], F32, kind="ExternalOutput").ap()
            dma("sync", ch_out[0], dp, PT.ap(), [pt_done, pv_pe_done])
            dma("sync", ch_out[0], dr, rowsB[0:1, :], [tok_rec])
        if name == "o" and lvl >= 1:
            do = nc.dram_tensor("d_o", [128, DC * T], BF16, kind="ExternalOutput").ap()
            dma("sync", ch_out[0], do, O_sb.ap(), [pv_done_dve])
        if name == "x1" and lvl >= 2:
            dx = nc.dram_tensor("d_x1", [128, 32 * T], BF16, kind="ExternalOutput").ap()
            dma("sync", ch_out[0], dx, x1.ap(), [(s_dve, s_dve.n)])
        if name == "mid" and lvl >= 3:
            dm = nc.dram_tensor("d_mid", [HCN, 128, T], BF16, kind="ExternalOutput").ap()
            dma("sync", ch_out[0], dm, mid_d[:], [(c, c.n) for c in ch_mid])
        if name == "y2" and lvl >= 4:
            da = nc.dram_tensor("d_y2", [128, 32 * T], BF16, kind="ExternalOutput").ap()
            dma("sync", ch_out[0], da, x1.ap(), [tok_stats2])

    final_waits = [(c, c.n) for c in [ch_vst] + ch_mid + ch_out]
    em.emit("sync", lambda e: None, final_waits)

    with nc.Block() as block:
        em.build(block)
    return nc


def _prep(inputs):
    x = np.asarray(inputs["L0_attn_x"], dtype=np.float32)
    Wqkv = np.asarray(inputs["L0_attn_Wqkv"], dtype=np.float32)
    bqkv = np.asarray(inputs["L0_attn_bqkv"], dtype=np.float32)
    Wo = np.asarray(inputs["L0_attn_Wo"], dtype=np.float32)
    bo = np.asarray(inputs["L0_attn_bo"], dtype=np.float32)
    wg = np.asarray(inputs["L0_ffn_w_gate"], dtype=np.int32)
    wu = np.asarray(inputs["L0_ffn_w_up"], dtype=np.int32)
    wd = np.asarray(inputs["L0_ffn_w_down"], dtype=np.int32)

    xt = [np.ascontiguousarray(x[b].T) for b in range(B)]
    com = {
        "wq_r": np.ascontiguousarray(
            Wqkv[:, 0:2048].reshape(CC, 128, 16, 128).transpose(2, 1, 0, 3)),
        "wv_r": np.ascontiguousarray(Wqkv[:, 2048:3072].reshape(CC, 128, DH)),
        "bqk_col": np.ascontiguousarray(bqkv[0:2048].reshape(16, 128).T),
        "bv_row": np.ascontiguousarray(bqkv[2048:3072])[None, :],
        "wo_r": np.ascontiguousarray(Wo.reshape(DC, 128, 32, 128).transpose(2, 1, 0, 3)),
        "bo_col": np.ascontiguousarray(bo.reshape(32, 128).T),
        "wg_r": np.ascontiguousarray(wg.reshape(HCN, 128, CC, 128).transpose(0, 3, 2, 1)),
        "wu_r": np.ascontiguousarray(wu.reshape(HCN, 128, CC, 128).transpose(0, 3, 2, 1)),
        "wd_r": np.ascontiguousarray(
            wd.reshape(4, 8, 128, HCN, 128).transpose(0, 3, 4, 1, 2)),
        "sg_in": np.ascontiguousarray(np.broadcast_to(
            np.float32(inputs["L0_ffn_scale_gate"]).reshape(1, 1), (128, 1))),
        "su_in": np.ascontiguousarray(np.broadcast_to(
            np.float32(inputs["L0_ffn_scale_up"]).reshape(1, 1), (128, 1))),
        "sd_in": np.ascontiguousarray(np.broadcast_to(
            np.float32(inputs["L0_ffn_scale_down"]).reshape(1, 1), (128, 1))),
    }
    in_maps = []
    for c in range(8):
        b, q = c // 4, c % 4
        m = dict(com)
        m["xt_full"] = xt[b]
        m["xt_own"] = np.ascontiguousarray(xt[b][:, q * T:(q + 1) * T])
        in_maps.append(m)
    return in_maps


def _run(inputs, dumps=(), upto="FIN"):
    in_maps = _prep(inputs)
    nc = build(dumps=dumps, upto=upto)
    return run_bass_kernel_spmd(nc, in_maps, core_ids=list(range(8)))


def kernel(**inputs):
    res = _run(inputs, dumps=_DEBUG_DUMPS)
    out = np.empty((B, S, CM), dtype=np.float32)
    for c in range(8):
        b, q = c // 4, c % 4
        out[b, q * T:(q + 1) * T, :] = res.results[c]["out"].T
    if _DEBUG_DUMPS:
        kernel._last = res
    return out


# version 09
# speedup vs baseline: 1.1795x; 1.1795x over previous
import numpy as np
import concourse.bass as bass
import concourse.mybir as mybir
from concourse.bass_utils import run_bass_kernel_spmd

F32 = mybir.dt.float32
BF16 = mybir.dt.bfloat16
I32 = mybir.dt.int32
AF = mybir.ActivationFunctionType
ALU = mybir.AluOpType

B = 2
S = 4096
CM = 4096
DH = 1024
T = 1024
CC = 32
DC = 8
KT = 32
HID = 11008
HCN = 86
SHIFT = 12.0
EPS = 1e-5
INV_CM = 1.0 / 4096.0

_DEBUG_DUMPS = ()




class _Sem:
    def __init__(self, nc, name):
        self.h = nc.alloc_semaphore(name)
        self.n = 0


class _Em:
    ENGS = ("tensor", "scalar", "vector", "gpsimd", "sync")

    def __init__(self, nc):
        self.nc = nc
        self.q = {e: [] for e in self.ENGS}

    def sem(self, name):
        return _Sem(self.nc, name)

    def emit(self, eng, fn, waits=(), inc=None, k=1):
        w = {}
        for t in waits:
            if t is None:
                continue
            s, v = t
            if v > w.get(id(s), (s, -1))[1]:
                w[id(s)] = (s, v)
        self.q[eng].append((list(w.values()), fn, inc, k))
        if inc is not None:
            inc.n += k
            return (inc, inc.n)
        return None

    def build(self, block):
        def mk(name):
            def body(eng):
                for waits, fn, inc, k in self.q[name]:
                    for (s, v) in waits:
                        eng.wait_ge(s.h, v)
                    ins = fn(eng)
                    if inc is not None:
                        ins.then_inc(inc.h, k)
            return body
        block.tensor(mk("tensor"))
        block.scalar(mk("scalar"))
        block.vector(mk("vector"))
        block.gpsimd(mk("gpsimd"))
        block.sync(mk("sync"))


def build(dumps=(), upto="FIN"):
    nc = bass.Bass("TRN2", target_bir_lowering=False)
    PH = ["A", "B", "C", "D1", "D2", "FIN"]
    lvl = PH.index(upto)

    xt_full = nc.dram_tensor("xt_full", [CM, S], F32, kind="ExternalInput").ap()
    xt_own = nc.dram_tensor("xt_own", [CM, T], F32, kind="ExternalInput").ap()
    wq_r = nc.dram_tensor("wq_r", [16, 128, CC, 128], F32, kind="ExternalInput").ap()
    wv_r = nc.dram_tensor("wv_r", [CC, 128, DH], F32, kind="ExternalInput").ap()
    bqk_col = nc.dram_tensor("bqk_col", [128, 16], F32, kind="ExternalInput").ap()
    bv_row = nc.dram_tensor("bv_row", [1, DH], F32, kind="ExternalInput").ap()
    wo_r = nc.dram_tensor("wo_r", [32, 128, DC, 128], F32, kind="ExternalInput").ap()
    bo_col = nc.dram_tensor("bo_col", [128, 32], F32, kind="ExternalInput").ap()
    wg_r = nc.dram_tensor("wg_r", [HCN, 128, CC, 128], I32, kind="ExternalInput").ap()
    wu_r = nc.dram_tensor("wu_r", [HCN, 128, CC, 128], I32, kind="ExternalInput").ap()
    wd_r = nc.dram_tensor("wd_r", [4, HCN, 128, 8, 128], I32, kind="ExternalInput").ap()
    sg_in = nc.dram_tensor("sg_in", [128, 1], F32, kind="ExternalInput").ap()
    su_in = nc.dram_tensor("su_in", [128, 1], F32, kind="ExternalInput").ap()
    sd_in = nc.dram_tensor("sd_in", [128, 1], F32, kind="ExternalInput").ap()

    out_t = nc.dram_tensor("out", [CM, T], F32, kind="ExternalOutput").ap()

    V_d = nc.dram_tensor("V_d", [KT, 128, DH], BF16, kind="Internal").ap()
    mid_d = nc.dram_tensor("mid_d", [HCN, 128, T], BF16, kind="Internal").ap()

    cdb = nc.alloc_sbuf_tensor("cdb", [128, 8], F32)
    for i, v in enumerate([0.03125, -SHIFT, EPS, INV_CM, 1.0, -0.5, -1.0]):
        nc.const_aps.register(F32, v, cdb[:, i:i + 1])
    ones_bf = nc.alloc_sbuf_tensor("ones_bf", [1, 512], BF16)
    ones_col = nc.alloc_sbuf_tensor("ones_col", [128, 1], BF16)
    ones_f32 = nc.alloc_sbuf_tensor("ones_f32", [1, 128], F32)
    sg_sb = nc.alloc_sbuf_tensor("sg_sb", [128, 1], F32)
    su_sb = nc.alloc_sbuf_tensor("su_sb", [128, 1], F32)
    sd_sb = nc.alloc_sbuf_tensor("sd_sb", [128, 1], F32)
    bqk_sb = nc.alloc_sbuf_tensor("bqk_sb", [128, 16], F32)
    bo_sb2 = nc.alloc_sbuf_tensor("bo_sb2", [128, 32], F32)

    def at(name, shape, dtype, off):
        end = off + int(np.prod(shape[1:])) * mybir.dt.size(dtype)
        assert off >= 28672 and end <= 229280, (name, off, end)
        return nc.alloc_sbuf_tensor_at(name, shape, dtype, offset=off)

    rowsA = at("rowsA", [1, 4096], F32, 28672)
    rowsB = at("rowsB", [1, 1024], F32, 45056)
    brows = at("brows", [1, 1024], BF16, 49152)
    RA = 63488
    RB = RA + 65536
    RC = RB + 65536
    RD = RC + 16384

    xq_bf = at("xq_bf", [128, CC * T], BF16, RA)
    PT = at("PT", [128, KT * T], BF16, RA)
    y1 = at("y1", [128, 32 * T], BF16, RA)
    wgbuf = at("wgbuf", [128, 2 * CC * 128], BF16, RA)
    wubuf = at("wubuf", [128, 2 * CC * 128], BF16, RA + 16384)
    gtbuf = at("gtbuf", [128, 2 * T], BF16, RD)
    utbuf = at("utbuf", [128, 2 * T], BF16, RD + 4096)
    acc = at("acc", [128, 8 * T], F32, RA)
    wdgbuf = at("wdgbuf", [128, 2 * 6 * 1024], BF16, RA + 32768)
    K_sb = at("K_sb", [128, DC * S], BF16, RB)
    x1 = at("x1", [128, 32 * T], BF16, RB)
    V_stage = at("V_stage", [128, 8 * DH], BF16, RC)
    Q_sb = at("Q_sb", [128, DC * T], BF16, RC)
    O_sb = at("O_sb", [128, DC * T], BF16, RC)
    mb_sb = at("mb_sb", [128, T], BF16, RC)
    rsb_sb = at("rsb_sb", [128, T], BF16, RC + 2048)
    t0buf = at("t0buf", [128, T], BF16, RC + 4096)
    t0gbuf = at("t0gbuf", [128, T], BF16, RC + 6144)
    midbufs = at("midbufs", [128, 4 * T], BF16, RD + 8192)
    midgbuf = at("midgbuf", [128, 2 * 6 * 1024], BF16, RC)
    m2b_sb = at("m2b_sb", [128, T], BF16, RC + 4096)
    r2b_sb = at("r2b_sb", [128, T], F32, RC + 8192)
    t02buf = at("t02buf", [128, T], BF16, RC + 12288)
    t02gbuf = at("t02gbuf", [128, T], BF16, RC + 14336)
    kslab = at("kslab", [128, 2 * 2048], BF16, RD)
    wvbuf = at("wvbuf", [128, 2 * 4 * 512], BF16, RD + 8192)
    vbuf = at("vbuf", [128, 2 * 2 * DH], BF16, RD)
    rb_sb = at("rb_sb", [128, T], F32, RD + 8192)
    wobuf = at("wobuf", [128, 2 * DH], BF16, RD)
    xcbuf = at("xcbuf", [128, T], F32, RD + 4096)
    sqbuf = at("sqbuf", [128, 2 * T], BF16, RD + 8192)
    y2buf = at("y2buf", [128, 4 * T], F32, RD)

    pp = [nc.alloc_psum_tensor(f"pp{i}", [128, 1024], F32) for i in range(4)]

    em = _Em(nc)
    s_pe = em.sem("pe")
    s_act = em.sem("act")
    s_dve = em.sem("dve")
    s_gp = em.sem("gp")

    def chan(name):
        return em.sem(name)

    def dma(q, ch, dst, src, waits=()):
        return em.emit(q, lambda e, d=dst, s=src: e.dma_start(d, s), waits, ch, 16)

    ch_binit = chan("binit")
    ch_sinit = chan("sinit")
    ch_xq = [chan(f"xq{i}") for i in range(8)]
    ch_ksl = [chan(f"ksl{i}") for i in range(2)]
    ch_wv = [chan(f"wv{i}") for i in range(2)]
    ch_vb = [chan(f"vb{i}") for i in range(2)]
    ch_wo = [chan(f"wo{i}") for i in range(2)]
    ch_xc = chan("xc")
    ch_wg = [chan(f"wg{i}") for i in range(2)]
    ch_wu = [chan(f"wu{i}") for i in range(2)]
    ch_wdg = [chan(f"wdg{i}") for i in range(2)]
    ch_mg = [chan(f"mg{i}") for i in range(2)]
    ch_vst = chan("vst")
    ch_mid = [chan(f"mst{i}") for i in range(4)]
    ch_out = [chan(f"outst{i}") for i in range(2)]

    def mm(o, l, r, st, sp, waits=(), inc=None):
        return em.emit("tensor",
                       lambda e, o=o, l=l, r=r, st=st, sp=sp:
                       e.matmul(o, l, r, start=st, stop=sp),
                       waits, s_pe if inc else None, 1)

    xq3 = xq_bf.ap().rearrange("p (cc t) -> p cc t", cc=CC)
    K3 = K_sb.ap().rearrange("p (dc t) -> p dc t", dc=DC)
    Q3 = Q_sb.ap().rearrange("p (dc t) -> p dc t", dc=DC)
    PT3 = PT.ap().rearrange("p (kt t) -> p kt t", kt=KT)
    O3 = O_sb.ap().rearrange("p (dc t) -> p dc t", dc=DC)
    y13 = y1.ap().rearrange("p (hc t) -> p hc t", hc=32)
    x13 = x1.ap().rearrange("p (cc t) -> p cc t", cc=32)
    acc3 = acc.ap().rearrange("p (dc t) -> p dc t", dc=8)
    mid4 = midbufs.ap().rearrange("p (i t) -> p i t", i=4)
    xt_own3 = xt_own.rearrange("(hc p) t -> hc p t", p=128)
    m_row = rowsA[0:1, 0:1024]
    v_row = rowsA[0:1, 1024:2048]
    r_row = rowsA[0:1, 2048:3072]
    tmp_row = rowsA[0:1, 3072:4096]

    def init_fn(e):
        e.memset(cdb[:, 0:1], 0.03125)
        e.memset(cdb[:, 1:2], -SHIFT)
        e.memset(cdb[:, 2:3], EPS)
        e.memset(cdb[:, 3:4], INV_CM)
        e.memset(cdb[:, 4:5], 1.0)
        e.memset(ones_bf[:], 1.0)
        e.memset(ones_col[:], 1.0)
        return e.memset(ones_f32[:], 1.0)
    tok_init = em.emit("vector", init_fn, (), s_dve, 1)
    tok_binit = dma("gpsimd", ch_binit, brows[0:1, 0:1024], bv_row)
    dma("sync", ch_sinit, sg_sb[:], sg_in)
    dma("sync", ch_sinit, su_sb[:], su_in)
    dma("sync", ch_sinit, sd_sb[:], sd_in)
    dma("sync", ch_sinit, bqk_sb[:], bqk_col)
    tok_sinit = dma("sync", ch_sinit, bo_sb2[:], bo_col)
    tok_bv = tok_binit
    tok_bq = tok_bk = tok_bo = tok_sg = tok_su = tok_sd = tok_sinit

    pp_gate = [tok_init] * 4

    xq_sub_gate = [None] * 8
    kslab_gate = [None, None]
    wv_gate = [None, None]
    vstage_gate = None

    def src_of(q):
        if q < 4:
            return xt_full.rearrange("(cc p) t -> p cc t", p=128)[:, :, q * T:(q + 1) * T]
        return xt_own.rearrange("(cc p) t -> p cc t", p=128)

    def load_sub(src3, sc):
        return dma("gpsimd", ch_xq[sc], xq3[:, sc * 4:(sc + 1) * 4, :],
                   src3[:, sc * 4:(sc + 1) * 4, :], [xq_sub_gate[sc]])

    def qk_section(toks_xq, slab0, bcol0, tok_b, dst3, evac0_waits=(), pre0=None):
        last = None
        for i in range(DC):
            sl = [(2 * i + h) % 2 for h in range(2)]
            if i == 0 and pre0 is not None:
                tok_sl = pre0
            else:
                tok_sl = [
                    dma("gpsimd", ch_ksl[sl[h]], kslab[:, sl[h] * 2048:(sl[h] + 1) * 2048],
                        wq_r[slab0 + i][:, h * 16:(h + 1) * 16, :].rearrange("p a b -> p (a b)"),
                        [kslab_gate[sl[h]]])
                    for h in range(2)
                ]
            ppi = i % 2
            ppt = pp[ppi]
            tok_pe = None
            for cc in range(CC):
                w = []
                if cc == 0:
                    w = [tok_sl[0], pp_gate[ppi], tok_init, tok_b]
                elif cc == 16:
                    w = [tok_sl[1]]
                if cc % 4 == 0:
                    w = w + [toks_xq[cc // 4]]
                lt = kslab[:, sl[cc // 16] * 2048 + (cc % 16) * 128:
                           sl[cc // 16] * 2048 + (cc % 16 + 1) * 128]
                for h in range(2):
                    tok = mm(ppt[:, h * 512:(h + 1) * 512], lt,
                             xq3[:, cc, h * 512:(h + 1) * 512], cc == 0, cc == CC - 1, w,
                             inc=(cc in (15, CC - 1) and h == 1))
                    w = []
                if cc == 15:
                    kslab_gate[sl[0]] = tok
                elif cc == CC - 1:
                    tok_pe = tok
            kslab_gate[sl[1]] = tok_pe
            tok_e = em.emit("scalar",
                            lambda e, d=dst3[:, i, :], s=ppt[:],
                                   b=bqk_sb[:, bcol0 + i: bcol0 + i + 1]:
                            e.activation(d, s, AF.Identity, bias=b),
                            [tok_pe] + (list(evac0_waits) if i == 0 else []), s_act, 1)
            pp_gate[ppi] = tok_e
            last = tok_pe
        return last

    src = src_of(0)
    toks_xq = [None] * 8
    toks_xq[0] = load_sub(src, 0)
    pre0 = [dma("gpsimd", ch_ksl[h], kslab[:, h * 2048:(h + 1) * 2048],
                wq_r[8][:, h * 16:(h + 1) * 16, :].rearrange("p a b -> p (a b)"))
            for h in range(2)]
    for sc in range(1, 8):
        toks_xq[sc] = load_sub(src, sc)
    for q in range(4):
        K_dst = K_sb.ap().rearrange("p (dc t) -> p dc t", dc=DC)[:, :, q * T:(q + 1) * T]
        qk_section(toks_xq, 8, 8, tok_bk, K_dst, pre0=pre0)
        next_src = src_of(q + 1)
        next_toks = [None] * 8
        next_pre0 = [None, None]
        tok_pe_v = None
        toks_ve = []
        wv4 = wvbuf.ap().rearrange("p (s j f) -> p s j f", s=2, j=4)
        for fh in range(2):
            for g in range(8):
                slot = g % 2
                tok_v = dma("gpsimd", ch_wv[slot], wv4[:, slot, :, :],
                            wv_r[4 * g:4 * g + 4, :, fh * 512:(fh + 1) * 512]
                            .rearrange("c p f -> p c f"), [wv_gate[slot]])
                w = [tok_v]
                if g == 0:
                    w = w + [tok_bv] + list(pp_gate)
                if fh == 0:
                    w = w + [toks_xq[g]]
                tok_cc = None
                for j in range(4):
                    cc = 4 * g + j
                    for tc in range(8):
                        tok_cc = mm(pp[tc // 2][:, (tc % 2) * 512:(tc % 2) * 512 + 512],
                                    xq3[:, cc, tc * 128:(tc + 1) * 128],
                                    wv4[:, slot, j, :],
                                    cc == 0, False, w, inc=(j == 3 and tc == 7))
                        w = []
                wv_gate[slot] = tok_cc
                if fh == 1:
                    xq_sub_gate[g] = tok_cc
                    if g >= 1:
                        next_toks[g - 1] = load_sub(next_src, g - 1)
                    if g in (2, 3):
                        nslab = 8 if q < 3 else 0
                        next_pre0[g - 2] = dma(
                            "gpsimd", ch_ksl[g - 2],
                            kslab[:, (g - 2) * 2048:(g - 1) * 2048],
                            wq_r[nslab][:, (g - 2) * 16:(g - 1) * 16, :].rearrange("p a b -> p (a b)"),
                            [kslab_gate[g - 2]])
            for tc in range(8):
                tok_pe_v = mm(pp[tc // 2][:, (tc % 2) * 512:(tc % 2) * 512 + 512],
                              ones_bf[0:1, 0:128],
                              brows[0:1, fh * 512:(fh + 1) * 512],
                              False, True, (), inc=(tc == 7))
            toks_ve = []
            for tc in range(8):
                toks_ve.append(em.emit(
                    "scalar",
                    lambda e, d=V_stage[:, tc * 1024 + fh * 512: tc * 1024 + fh * 512 + 512],
                           s=pp[tc // 2][:, (tc % 2) * 512:(tc % 2) * 512 + 512]:
                    e.copy(d, s),
                    [tok_pe_v] + ([vstage_gate] if (fh == 0 and tc == 0) else []),
                    s_act, 1))
            for i in range(4):
                pp_gate[i] = toks_ve[2 * i + 1]
        tok_vst = dma("sync", ch_vst, V_d[q * 8:(q + 1) * 8].rearrange("tc p f -> p tc f"),
                      V_stage.ap().rearrange("p (tc f) -> p tc f", tc=8),
                      [toks_ve[-1]])
        next_toks[7] = load_sub(next_src, 7)
        toks_xq = next_toks
        pre0 = next_pre0 if next_pre0[0] is not None else None
        vstage_gate = tok_vst
    tok_q_last = qk_section(toks_xq, 0, 0, tok_bq, Q3, evac0_waits=[vstage_gate], pre0=pre0)
    a_done_act = (s_act, s_act.n)

    pt_done = tok_rec = pv_done_dve = pv_pe_done = None
    if lvl >= 1:
        exp_toks = []
        for kt in range(KT):
            ppi = kt % 2
            ppt = pp[ppi]
            w = [pp_gate[ppi]] + ([a_done_act] if kt == 0 else [])
            tok_s = None
            for dc in range(DC):
                for qh in range(2):
                    tok_s = mm(ppt[:, qh * 512:(qh + 1) * 512],
                               K3[:, dc, kt * 128:(kt + 1) * 128],
                               Q3[:, dc, qh * 512:(qh + 1) * 512],
                               dc == 0, dc == DC - 1, w,
                               inc=(dc == DC - 1 and qh == 1))
                    w = []
            tok_e = em.emit("scalar",
                            lambda e, d=PT3[:, kt, :], s=ppt[:]:
                            e.activation(d, s, AF.Exp, bias=-SHIFT, scale=0.03125),
                            [tok_s], s_act, 1)
            pp_gate[ppi] = tok_e
            exp_toks.append(tok_e)
            if kt >= 1:
                for qh in range(2):
                    w = [exp_toks[kt - 1]] + ([pp_gate[2]] if kt == 1 else [])
                    mm(pp[2][0:1, qh * 512:(qh + 1) * 512], ones_col[:, 0:1],
                       PT3[:, kt - 1, qh * 512:(qh + 1) * 512],
                       kt == 1, False, w)
        pt_done = (s_act, s_act.n)
        for qh in range(2):
            tok_sums = mm(pp[2][0:1, qh * 512:(qh + 1) * 512], ones_col[:, 0:1],
                          PT3[:, KT - 1, qh * 512:(qh + 1) * 512],
                          False, True, [exp_toks[KT - 1]], inc=(qh == 1))
        def rec_fn(e):
            e.activation(rowsA[0:1, 0:512], pp[2][0:1, 0:512], AF.Ln)
            e.activation(rowsA[0:1, 512:1024], pp[2][0:1, 512:1024], AF.Ln)
            e.activation(rowsB[0:1, 0:512], rowsA[0:1, 0:512], AF.Exp, scale=-1.0)
            return e.activation(rowsB[0:1, 512:1024], rowsA[0:1, 512:1024], AF.Exp, scale=-1.0)
        tok_rec = em.emit("scalar", rec_fn, [tok_sums], s_act, 1)
        pp_gate[2] = tok_rec
        tok_bc = None
        for qh in range(2):
            tok_bc = mm(pp[3][:, qh * 512:(qh + 1) * 512], ones_f32[0:1, :],
                        rowsB[0:1, qh * 512:(qh + 1) * 512], True, True,
                        [tok_rec, pp_gate[3]] if qh == 0 else [], inc=(qh == 1))
        tok_rb = em.emit("scalar", lambda e: e.copy(rb_sb[:], pp[3][:]),
                         [tok_bc], s_act, 1)
        pp_gate[3] = tok_rb
        vb_gate = [tok_q_last] * 2
        vb4 = vbuf.ap().rearrange("p (s k f) -> p s k f", s=2, k=2)
        for qh in range(2):
            tok_pe_pv = None
            for kp in range(KT // 2):
                slot = kp % 2
                tok_v = dma("sync", ch_vb[slot], vb4[:, slot, :, :],
                            V_d[2 * kp:2 * kp + 2].rearrange("k p f -> p k f"),
                            [vb_gate[slot]] + ([vstage_gate] if kp == 0 else []))
                w = [tok_v] + (list(pp_gate) if kp == 0 else [])
                for ki in range(2):
                    kc = 2 * kp + ki
                    for dc in range(DC):
                        tok_pe_pv = mm(pp[dc // 2][:, (dc % 2) * 512:(dc % 2) * 512 + 512],
                                       vb4[:, slot, ki, dc * 128:(dc + 1) * 128],
                                       PT3[:, kc, qh * 512:(qh + 1) * 512],
                                       kc == 0, kc == KT - 1, w,
                                       inc=(ki == 1 and dc == DC - 1))
                        w = []
                vb_gate[slot] = tok_pe_pv
            toks = []
            for dc in range(8):
                toks.append(em.emit(
                    "vector",
                    lambda e, d=O3[:, dc, qh * 512:(qh + 1) * 512],
                           s=pp[dc // 2][:, (dc % 2) * 512:(dc % 2) * 512 + 512],
                           rb=rb_sb[:, qh * 512:(qh + 1) * 512]:
                    e.scalar_tensor_tensor(d, s, 1.0, rb, ALU.mult, ALU.mult),
                    [tok_pe_pv, tok_rb], s_dve, 1))
            for i in range(4):
                pp_gate[i] = toks[2 * i + 1]
        pv_done_dve = (s_dve, s_dve.n)
        pv_pe_done = (s_pe, s_pe.n)

    def ln_rows(tok_in, mb_dst, rb_dst, gates):
        def rows_fn(e):
            e.tensor_scalar_mul(m_row[0:1, 0:512], pp[2][0:1, 0:512], INV_CM)
            e.tensor_scalar_mul(m_row[0:1, 512:1024], pp[2][0:1, 512:1024], INV_CM)
            e.tensor_scalar_mul(v_row[0:1, 0:512], pp[3][0:1, 0:512], INV_CM)
            e.tensor_scalar_mul(v_row[0:1, 512:1024], pp[3][0:1, 512:1024], INV_CM)
            e.tensor_tensor(tmp_row, m_row, m_row, ALU.mult)
            return e.tensor_tensor(v_row, v_row, tmp_row, ALU.subtract)
        tok_rows = em.emit("vector", rows_fn, [tok_in], s_dve, 1)
        def rstd_fn(e):
            e.activation(tmp_row, v_row, AF.Ln, bias=EPS)
            return e.activation(r_row, tmp_row, AF.Exp, scale=-0.5)
        tok_r = em.emit("scalar", rstd_fn, [tok_rows], s_act, 1)
        tok_bc = None
        for i, src_row in enumerate((m_row, r_row)):
            for qh in range(2):
                tok_bc = mm(pp[i][:, qh * 512:(qh + 1) * 512], ones_f32[0:1, :],
                            src_row[0:1, qh * 512:(qh + 1) * 512], True, True,
                            ([tok_r] + gates) if (i == 0 and qh == 0) else [],
                            inc=(i == 1 and qh == 1))
        def cp_fn(e):
            e.copy(mb_dst[:], pp[0][:])
            return e.copy(rb_dst[:], pp[1][:])
        tok_mb = em.emit("scalar", cp_fn, [tok_bc], s_act, 1)
        return tok_mb

    tok_stats = tok_mb = x1_done = None
    if lvl >= 2:
        wo_gate = [pv_done_dve, pv_done_dve]
        xc_gate = [pv_done_dve]
        sq_gate = [None, None]
        for hc in range(32):
            slot = hc % 2
            tok_w = dma("gpsimd", ch_wo[slot], wobuf[:, slot * DH:(slot + 1) * DH],
                        wo_r[hc].rearrange("p a b -> p (a b)"), [wo_gate[slot]])
            tok_x = dma("sync", ch_xc, xcbuf[:], xt_own3[hc], [xc_gate[0]])
            ppi = hc % 2
            ppt = pp[ppi]
            w = [tok_w, pp_gate[ppi], tok_bo] + ([pv_done_dve] if hc == 0 else [])
            tok_y = None
            for dc in range(DC):
                for qh in range(2):
                    tok_y = mm(ppt[:, qh * 512:(qh + 1) * 512],
                               wobuf[:, slot * DH + dc * 128: slot * DH + (dc + 1) * 128],
                               O3[:, dc, qh * 512:(qh + 1) * 512], dc == 0, dc == DC - 1, w,
                               inc=(dc == DC - 1 and qh == 1))
                    w = []
            wo_gate[slot] = tok_y
            tok_add = em.emit("vector",
                              lambda e, d=y13[:, hc, :], s=ppt[:], x=xcbuf[:],
                                     b=bo_sb2[:, hc:hc + 1]:
                              e.scalar_tensor_tensor(d, s, b, x, ALU.add, ALU.add),
                              [tok_y, tok_x], s_dve, 1)
            xc_gate[0] = tok_add
            pp_gate[ppi] = tok_add
            tok_sc = em.emit("vector",
                             lambda e, hc=hc, slot=slot:
                             e.tensor_mul(sqbuf[:, slot * T:(slot + 1) * T],
                                          y13[:, hc, :], y13[:, hc, :]),
                             [tok_add, sq_gate[slot]], s_dve, 1)
            w = [tok_add, tok_sc] + ([pp_gate[2], pp_gate[3]] if hc == 0 else [])
            for qh in range(2):
                mm(pp[2][0:1, qh * 512:(qh + 1) * 512], ones_col[:, 0:1],
                   y13[:, hc, qh * 512:(qh + 1) * 512],
                   hc == 0, hc == 31, w)
                w = []
                tok_stats = mm(pp[3][0:1, qh * 512:(qh + 1) * 512], ones_col[:, 0:1],
                               sqbuf[:, slot * T + qh * 512: slot * T + qh * 512 + 512],
                               hc == 0, hc == 31, (), inc=(qh == 1))
            sq_gate[slot] = tok_stats

        tok_mb = ln_rows(tok_stats, mb_sb, rsb_sb, [pp_gate[0], pp_gate[1]])
        for i in range(4):
            pp_gate[i] = tok_mb
        x1_toks = []
        for hc in range(32):
            def ap_fn(e, hc=hc):
                e.tensor_tensor(t0buf[:], y13[:, hc, :], mb_sb[:], ALU.subtract)
                return e.tensor_tensor(x13[:, hc, :], t0buf[:], rsb_sb[:], ALU.mult)
            x1_toks.append(em.emit("vector", ap_fn, [tok_mb] if hc == 0 else [], s_dve, 1))
        x1_done = x1_toks[-1]

    mid_store = [None] * HCN
    d1_last_mid = None
    if lvl >= 3:
        wg_gate = [x1_toks[3], x1_toks[7]]
        wu_gate = [x1_toks[11], x1_toks[15]]
        gt_gate = [None, None]
        mid_gate = [None, None, None, None]
        for hc in range(HCN):
            slot = hc % 2
            tok_g = dma("gpsimd", ch_wg[slot], wgbuf[:, slot * 4096:(slot + 1) * 4096],
                        wg_r[hc].rearrange("p a b -> p (a b)"), [wg_gate[slot]])
            tok_u = dma("gpsimd", ch_wu[slot], wubuf[:, slot * 4096:(slot + 1) * 4096],
                        wu_r[hc].rearrange("p a b -> p (a b)"), [wu_gate[slot]])
            ppg = pp[hc % 2]
            ppu = pp[2 + hc % 2]
            w = [tok_g, pp_gate[hc % 2]] + ([x1_done] if hc == 0 else [])
            tok_gmm = None
            for cc in range(CC):
                for qh in range(2):
                    tok_gmm = mm(ppg[:, qh * 512:(qh + 1) * 512],
                                 wgbuf[:, slot * 4096 + cc * 128: slot * 4096 + (cc + 1) * 128],
                                 x13[:, cc, qh * 512:(qh + 1) * 512],
                                 cc == 0, cc == CC - 1, w,
                                 inc=(cc == CC - 1 and qh == 1))
                    w = []
            w = [tok_u, pp_gate[2 + hc % 2]]
            tok_umm = None
            for cc in range(CC):
                for qh in range(2):
                    tok_umm = mm(ppu[:, qh * 512:(qh + 1) * 512],
                                 wubuf[:, slot * 4096 + cc * 128: slot * 4096 + (cc + 1) * 128],
                                 x13[:, cc, qh * 512:(qh + 1) * 512],
                                 cc == 0, cc == CC - 1, w,
                                 inc=(cc == CC - 1 and qh == 1))
                    w = []
            wg_gate[slot] = tok_gmm
            wu_gate[slot] = tok_umm
            def ev_fn(e, slot=slot, ppg=ppg, ppu=ppu):
                e.activation(gtbuf[:, slot * T:(slot + 1) * T], ppg[:], AF.Silu,
                             scale=sg_sb[:, 0:1])
                return e.activation(utbuf[:, slot * T:(slot + 1) * T], ppu[:],
                                    AF.Identity, scale=su_sb[:, 0:1])
            tok_si = em.emit("scalar", ev_fn,
                             [tok_gmm, tok_umm, gt_gate[slot], tok_sg, tok_su],
                             s_act, 1)
            pp_gate[hc % 2] = tok_si
            pp_gate[2 + hc % 2] = tok_si
            ms = hc % 4
            tok_mid = em.emit("vector",
                              lambda e, d=mid4[:, ms, :],
                                     a=gtbuf[:, slot * T:(slot + 1) * T],
                                     b=utbuf[:, slot * T:(slot + 1) * T]:
                              e.tensor_tensor(d, a, b, ALU.mult),
                              [tok_si, mid_gate[ms]], s_dve, 1)
            gt_gate[slot] = tok_mid
            d1_last_mid = tok_mid
            tok_mst = dma("sync", ch_mid[ms], mid_d[hc], mid4[:, ms, :], [tok_mid])
            mid_gate[ms] = tok_mst
            mid_store[hc] = tok_mst

    if lvl >= 4:
        GDEF = 6
        gstarts = list(range(0, 84, GDEF)) + [84]
        gsizes = [GDEF] * 14 + [2]
        wdg3 = wdgbuf.ap().rearrange("p (s g x) -> p s g x", s=2, g=GDEF)
        midg3 = midgbuf.ap().rearrange("p (s g t) -> p s g t", s=2, g=GDEF)
        wdg_gate = [x1_toks[21], x1_toks[27]]
        midg_gate = [None, None]
        y2b_gate = [None, None]
        tok_stats2 = None
        for qtr in range(4):
            acc_tok = [None] * 8
            for gi, (g0, G) in enumerate(zip(gstarts, gsizes)):
                slot = gi % 2
                tok_wd = dma("gpsimd", ch_wdg[slot], wdg3[:, slot, 0:G, :],
                             wd_r[qtr, g0:g0 + G].rearrange("g p a b -> p g (a b)"),
                             [wdg_gate[slot]])
                tok_md = dma("scalar", ch_mg[slot], midg3[:, slot, 0:G, :],
                             mid_d[g0:g0 + G].rearrange("g p t -> p g t"),
                             [midg_gate[slot]] + [mid_store[h] for h in range(g0, g0 + G)])
                last_pe = None
                for dc in range(8):
                    ppi = dc % 2
                    ppt = pp[ppi]
                    w = [pp_gate[ppi]] + ([tok_wd, tok_md] if dc < 2 else [])
                    tok_d = None
                    for i in range(G):
                        for qh in range(2):
                            tok_d = mm(ppt[:, qh * 512:(qh + 1) * 512],
                                       wdg3[:, slot, i, dc * 128:(dc + 1) * 128],
                                       midg3[:, slot, i, qh * 512:(qh + 1) * 512],
                                       i == 0, i == G - 1, w,
                                       inc=(i == G - 1 and qh == 1))
                            w = []
                    last_pe = tok_d
                    if gi == 0:
                        tok_a = em.emit("vector",
                                        lambda e, d=acc3[:, dc, :], s=ppt[:]:
                                        e.tensor_copy(d, s), [tok_d], s_dve, 1)
                    else:
                        tok_a = em.emit("vector",
                                        lambda e, d=acc3[:, dc, :], s=ppt[:]:
                                        e.tensor_tensor(d, s, d, ALU.add),
                                        [tok_d], s_dve, 1)
                    pp_gate[ppi] = tok_a
                    acc_tok[dc] = tok_a
                wdg_gate[slot] = last_pe
                midg_gate[slot] = last_pe
            for dc in range(8):
                ch = qtr * 8 + dc
                slot = dc % 2
                tok_y2 = em.emit("vector",
                                 lambda e, d=x13[:, ch, :],
                                        a=acc3[:, dc, :], x=x13[:, ch, :]:
                                 e.scalar_tensor_tensor(d, a, sd_sb[:, 0:1], x,
                                                        ALU.mult, ALU.add),
                                 [acc_tok[dc], tok_sd], s_dve, 1)
                tok_sc2 = em.emit("vector",
                                  lambda e, slot=slot, ch=ch:
                                  e.tensor_mul(sqbuf[:, slot * T:(slot + 1) * T],
                                               x13[:, ch, :], x13[:, ch, :]),
                                  [tok_y2, y2b_gate[slot]] +
                                  ([tok_stats, sq_gate[0], sq_gate[1]] if ch == 0 else []),
                                  s_dve, 1)
                w = [tok_y2, tok_sc2] + ([pp_gate[2], pp_gate[3]] if ch == 0 else [])
                for qh in range(2):
                    mm(pp[2][0:1, qh * 512:(qh + 1) * 512], ones_col[:, 0:1],
                       x13[:, ch, qh * 512:(qh + 1) * 512],
                       ch == 0, ch == 31, w)
                    w = []
                    tok_stats2 = mm(pp[3][0:1, qh * 512:(qh + 1) * 512], ones_col[:, 0:1],
                                    sqbuf[:, slot * T + qh * 512: slot * T + qh * 512 + 512],
                                    ch == 0, ch == 31, (), inc=(qh == 1))
                y2b_gate[slot] = tok_stats2

        tok_m2b = ln_rows(tok_stats2, m2b_sb, r2b_sb, [pp_gate[0], pp_gate[1]])
        ob_gate = [None, None, None, None]
        ch_out = ch_out + [chan("outst2"), chan("outst3")]
        out3 = out_t.rearrange("(hc p) t -> hc p t", p=128)
        for ch in range(32):
            slot = ch % 4
            eng = "gpsimd" if ch % 6 == 5 else "vector"
            t02 = t02gbuf if eng == "gpsimd" else t02buf
            def ap2_fn(e, slot=slot, ch=ch, t02=t02):
                e.tensor_tensor(t02[:], x13[:, ch, :], m2b_sb[:], ALU.subtract)
                return e.tensor_tensor(y2buf[:, slot * T:(slot + 1) * T], t02[:],
                                       r2b_sb[:], ALU.mult)
            tok_o = em.emit(eng, ap2_fn, [ob_gate[slot], tok_m2b],
                            s_gp if eng == "gpsimd" else s_dve, 1)
            tok_out = dma("sync", ch_out[slot], out3[ch],
                          y2buf[:, slot * T:(slot + 1) * T], [tok_o])
            ob_gate[slot] = tok_out

    for name in dumps:
        if name == "qkv":
            dq = nc.dram_tensor("d_q", [128, DC * T], BF16, kind="ExternalOutput").ap()
            dk = nc.dram_tensor("d_k", [128, DC * S], BF16, kind="ExternalOutput").ap()
            dv = nc.dram_tensor("d_v", [KT, 128, DH], BF16, kind="ExternalOutput").ap()
            dma("sync", ch_out[0], dq, Q_sb.ap(), [a_done_act])
            dma("sync", ch_out[0], dk, K_sb.ap(), [a_done_act])
            dma("sync", ch_out[0], dv, V_d[:], [vstage_gate])
        if name == "pt" and lvl >= 1:
            dp = nc.dram_tensor("d_pt", [128, KT * T], BF16, kind="ExternalOutput").ap()
            dr = nc.dram_tensor("d_rec", [1, 1024], F32, kind="ExternalOutput").ap()
            dma("sync", ch_out[0], dp, PT.ap(), [pt_done, pv_pe_done])
            dma("sync", ch_out[0], dr, rowsB[0:1, :], [tok_rec])
        if name == "o" and lvl >= 1:
            do = nc.dram_tensor("d_o", [128, DC * T], BF16, kind="ExternalOutput").ap()
            dma("sync", ch_out[0], do, O_sb.ap(), [pv_done_dve])
        if name == "x1" and lvl >= 2:
            dx = nc.dram_tensor("d_x1", [128, 32 * T], BF16, kind="ExternalOutput").ap()
            dma("sync", ch_out[0], dx, x1.ap(), [(s_dve, s_dve.n)])
        if name == "mid" and lvl >= 3:
            dm = nc.dram_tensor("d_mid", [HCN, 128, T], BF16, kind="ExternalOutput").ap()
            dma("sync", ch_out[0], dm, mid_d[:], [(c, c.n) for c in ch_mid])
        if name == "y2" and lvl >= 4:
            da = nc.dram_tensor("d_y2", [128, 32 * T], BF16, kind="ExternalOutput").ap()
            dma("sync", ch_out[0], da, x1.ap(), [tok_stats2])

    final_waits = [(c, c.n) for c in [ch_vst] + ch_mid + ch_out]
    em.emit("sync", lambda e: None, final_waits)

    with nc.Block() as block:
        em.build(block)
    return nc


def _prep(inputs):
    x = np.asarray(inputs["L0_attn_x"], dtype=np.float32)
    Wqkv = np.asarray(inputs["L0_attn_Wqkv"], dtype=np.float32)
    bqkv = np.asarray(inputs["L0_attn_bqkv"], dtype=np.float32)
    Wo = np.asarray(inputs["L0_attn_Wo"], dtype=np.float32)
    bo = np.asarray(inputs["L0_attn_bo"], dtype=np.float32)
    wg = np.asarray(inputs["L0_ffn_w_gate"], dtype=np.int32)
    wu = np.asarray(inputs["L0_ffn_w_up"], dtype=np.int32)
    wd = np.asarray(inputs["L0_ffn_w_down"], dtype=np.int32)

    xt = [np.ascontiguousarray(x[b].T) for b in range(B)]
    com = {
        "wq_r": np.ascontiguousarray(
            Wqkv[:, 0:2048].reshape(CC, 128, 16, 128).transpose(2, 1, 0, 3)),
        "wv_r": np.ascontiguousarray(Wqkv[:, 2048:3072].reshape(CC, 128, DH)),
        "bqk_col": np.ascontiguousarray(bqkv[0:2048].reshape(16, 128).T),
        "bv_row": np.ascontiguousarray(bqkv[2048:3072])[None, :],
        "wo_r": np.ascontiguousarray(Wo.reshape(DC, 128, 32, 128).transpose(2, 1, 0, 3)),
        "bo_col": np.ascontiguousarray(bo.reshape(32, 128).T),
        "wg_r": np.ascontiguousarray(wg.reshape(HCN, 128, CC, 128).transpose(0, 3, 2, 1)),
        "wu_r": np.ascontiguousarray(wu.reshape(HCN, 128, CC, 128).transpose(0, 3, 2, 1)),
        "wd_r": np.ascontiguousarray(
            wd.reshape(4, 8, 128, HCN, 128).transpose(0, 3, 4, 1, 2)),
        "sg_in": np.ascontiguousarray(np.broadcast_to(
            np.float32(inputs["L0_ffn_scale_gate"]).reshape(1, 1), (128, 1))),
        "su_in": np.ascontiguousarray(np.broadcast_to(
            np.float32(inputs["L0_ffn_scale_up"]).reshape(1, 1), (128, 1))),
        "sd_in": np.ascontiguousarray(np.broadcast_to(
            np.float32(inputs["L0_ffn_scale_down"]).reshape(1, 1), (128, 1))),
    }
    in_maps = []
    for c in range(8):
        b, q = c // 4, c % 4
        m = dict(com)
        m["xt_full"] = xt[b]
        m["xt_own"] = np.ascontiguousarray(xt[b][:, q * T:(q + 1) * T])
        in_maps.append(m)
    return in_maps


def _run(inputs, dumps=(), upto="FIN"):
    in_maps = _prep(inputs)
    nc = build(dumps=dumps, upto=upto)
    return run_bass_kernel_spmd(nc, in_maps, core_ids=list(range(8)))


def kernel(**inputs):
    res = _run(inputs, dumps=_DEBUG_DUMPS)
    out = np.empty((B, S, CM), dtype=np.float32)
    for c in range(8):
        b, q = c // 4, c % 4
        out[b, q * T:(q + 1) * T, :] = res.results[c]["out"].T
    if _DEBUG_DUMPS:
        kernel._last = res
    return out


# version 10 (final)
# speedup vs baseline: 1.1810x; 1.0012x over previous
import numpy as np
import concourse.bass as bass
import concourse.mybir as mybir
from concourse.bass_utils import run_bass_kernel_spmd

F32 = mybir.dt.float32
BF16 = mybir.dt.bfloat16
I32 = mybir.dt.int32
AF = mybir.ActivationFunctionType
ALU = mybir.AluOpType

B = 2
S = 4096
CM = 4096
DH = 1024
T = 1024
CC = 32
DC = 8
KT = 32
HID = 11008
HCN = 86
SHIFT = 12.0
EPS = 1e-5
INV_CM = 1.0 / 4096.0

_DEBUG_DUMPS = ()




class _Sem:
    def __init__(self, nc, name):
        self.h = nc.alloc_semaphore(name)
        self.n = 0


class _Em:
    ENGS = ("tensor", "scalar", "vector", "gpsimd", "sync")

    def __init__(self, nc):
        self.nc = nc
        self.q = {e: [] for e in self.ENGS}

    def sem(self, name):
        return _Sem(self.nc, name)

    def emit(self, eng, fn, waits=(), inc=None, k=1):
        w = {}
        for t in waits:
            if t is None:
                continue
            s, v = t
            if v > w.get(id(s), (s, -1))[1]:
                w[id(s)] = (s, v)
        self.q[eng].append((list(w.values()), fn, inc, k))
        if inc is not None:
            inc.n += k
            return (inc, inc.n)
        return None

    def build(self, block):
        def mk(name):
            def body(eng):
                for waits, fn, inc, k in self.q[name]:
                    for (s, v) in waits:
                        eng.wait_ge(s.h, v)
                    ins = fn(eng)
                    if inc is not None:
                        ins.then_inc(inc.h, k)
            return body
        block.tensor(mk("tensor"))
        block.scalar(mk("scalar"))
        block.vector(mk("vector"))
        block.gpsimd(mk("gpsimd"))
        block.sync(mk("sync"))


def build(dumps=(), upto="FIN"):
    nc = bass.Bass("TRN2", target_bir_lowering=False)
    PH = ["A", "B", "C", "D1", "D2", "FIN"]
    lvl = PH.index(upto)

    xt_full = nc.dram_tensor("xt_full", [CM, S], F32, kind="ExternalInput").ap()
    xt_own = nc.dram_tensor("xt_own", [CM, T], F32, kind="ExternalInput").ap()
    wq_r = nc.dram_tensor("wq_r", [16, 128, CC, 128], F32, kind="ExternalInput").ap()
    wv_r = nc.dram_tensor("wv_r", [CC, 128, DH], F32, kind="ExternalInput").ap()
    bqk_col = nc.dram_tensor("bqk_col", [128, 16], F32, kind="ExternalInput").ap()
    bv_row = nc.dram_tensor("bv_row", [1, DH], F32, kind="ExternalInput").ap()
    wo_r = nc.dram_tensor("wo_r", [32, 128, DC, 128], F32, kind="ExternalInput").ap()
    bo_col = nc.dram_tensor("bo_col", [128, 32], F32, kind="ExternalInput").ap()
    wg_r = nc.dram_tensor("wg_r", [HCN, 128, CC, 128], I32, kind="ExternalInput").ap()
    wu_r = nc.dram_tensor("wu_r", [HCN, 128, CC, 128], I32, kind="ExternalInput").ap()
    wd_r = nc.dram_tensor("wd_r", [4, HCN, 128, 8, 128], I32, kind="ExternalInput").ap()
    sg_in = nc.dram_tensor("sg_in", [128, 1], F32, kind="ExternalInput").ap()
    su_in = nc.dram_tensor("su_in", [128, 1], F32, kind="ExternalInput").ap()
    sd_in = nc.dram_tensor("sd_in", [128, 1], F32, kind="ExternalInput").ap()

    out_t = nc.dram_tensor("out", [CM, T], F32, kind="ExternalOutput").ap()

    V_d = nc.dram_tensor("V_d", [KT, 128, DH], BF16, kind="Internal").ap()
    mid_d = nc.dram_tensor("mid_d", [HCN, 128, T], BF16, kind="Internal").ap()

    cdb = nc.alloc_sbuf_tensor("cdb", [128, 8], F32)
    for i, v in enumerate([0.03125, -SHIFT, EPS, INV_CM, 1.0, -0.5, -1.0]):
        nc.const_aps.register(F32, v, cdb[:, i:i + 1])
    ones_bf = nc.alloc_sbuf_tensor("ones_bf", [1, 512], BF16)
    ones_col = nc.alloc_sbuf_tensor("ones_col", [128, 1], BF16)
    ones_f32 = nc.alloc_sbuf_tensor("ones_f32", [1, 128], F32)
    sg_sb = nc.alloc_sbuf_tensor("sg_sb", [128, 1], F32)
    su_sb = nc.alloc_sbuf_tensor("su_sb", [128, 1], F32)
    sd_sb = nc.alloc_sbuf_tensor("sd_sb", [128, 1], F32)
    bqk_sb = nc.alloc_sbuf_tensor("bqk_sb", [128, 16], F32)
    bo_sb2 = nc.alloc_sbuf_tensor("bo_sb2", [128, 32], F32)

    def at(name, shape, dtype, off):
        end = off + int(np.prod(shape[1:])) * mybir.dt.size(dtype)
        assert off >= 28672 and end <= 229280, (name, off, end)
        return nc.alloc_sbuf_tensor_at(name, shape, dtype, offset=off)

    rowsA = at("rowsA", [1, 4096], F32, 28672)
    rowsB = at("rowsB", [1, 1024], F32, 45056)
    brows = at("brows", [1, 1024], BF16, 49152)
    RA = 63488
    RB = RA + 65536
    RC = RB + 65536
    RD = RC + 16384

    xq_bf = at("xq_bf", [128, CC * T], BF16, RA)
    PT = at("PT", [128, KT * T], BF16, RA)
    y1 = at("y1", [128, 32 * T], BF16, RA)
    wgbuf = at("wgbuf", [128, 2 * CC * 128], BF16, RA)
    wubuf = at("wubuf", [128, 2 * CC * 128], BF16, RA + 16384)
    gtbuf = at("gtbuf", [128, 2 * T], BF16, RD)
    utbuf = at("utbuf", [128, 2 * T], BF16, RD + 4096)
    acc = at("acc", [128, 8 * T], F32, RA)
    wdgbuf = at("wdgbuf", [128, 2 * 6 * 1024], BF16, RA + 32768)
    K_sb = at("K_sb", [128, DC * S], BF16, RB)
    x1 = at("x1", [128, 32 * T], BF16, RB)
    V_stage = at("V_stage", [128, 8 * DH], BF16, RC)
    Q_sb = at("Q_sb", [128, DC * T], BF16, RC)
    O_sb = at("O_sb", [128, DC * T], BF16, RC)
    mb_sb = at("mb_sb", [128, T], BF16, RC)
    rsb_sb = at("rsb_sb", [128, T], BF16, RC + 2048)
    t0buf = at("t0buf", [128, T], BF16, RC + 4096)
    t0gbuf = at("t0gbuf", [128, T], BF16, RC + 6144)
    midbufs = at("midbufs", [128, 4 * T], BF16, RD + 8192)
    midgbuf = at("midgbuf", [128, 2 * 6 * 1024], BF16, RC)
    m2b_sb = at("m2b_sb", [128, T], BF16, RC + 4096)
    r2b_sb = at("r2b_sb", [128, T], F32, RC + 8192)
    t02buf = at("t02buf", [128, T], BF16, RC + 12288)
    t02gbuf = at("t02gbuf", [128, T], BF16, RC + 14336)
    kslab = at("kslab", [128, 2 * 2048], BF16, RD)
    wvbuf = at("wvbuf", [128, 2 * 4 * 512], BF16, RD + 8192)
    vbuf = at("vbuf", [128, 2 * 2 * DH], BF16, RD)
    rb_sb = at("rb_sb", [128, T], F32, RD + 8192)
    wobuf = at("wobuf", [128, 2 * DH], BF16, RD)
    xcbuf = at("xcbuf", [128, 2 * T], F32, RD + 4096)
    sqbuf = at("sqbuf", [128, 2 * T], BF16, RD + 12288)
    y2buf = at("y2buf", [128, 4 * T], F32, RD)

    pp = [nc.alloc_psum_tensor(f"pp{i}", [128, 1024], F32) for i in range(4)]

    em = _Em(nc)
    s_pe = em.sem("pe")
    s_act = em.sem("act")
    s_dve = em.sem("dve")
    s_gp = em.sem("gp")

    def chan(name):
        return em.sem(name)

    def dma(q, ch, dst, src, waits=()):
        return em.emit(q, lambda e, d=dst, s=src: e.dma_start(d, s), waits, ch, 16)

    ch_binit = chan("binit")
    ch_sinit = chan("sinit")
    ch_xq = [chan(f"xq{i}") for i in range(8)]
    ch_ksl = [chan(f"ksl{i}") for i in range(2)]
    ch_wv = [chan(f"wv{i}") for i in range(2)]
    ch_vb = [chan(f"vb{i}") for i in range(2)]
    ch_wo = [chan(f"wo{i}") for i in range(2)]
    ch_xc = [chan("xc0"), chan("xc1")]
    ch_wg = [chan(f"wg{i}") for i in range(2)]
    ch_wu = [chan(f"wu{i}") for i in range(2)]
    ch_wdg = [chan(f"wdg{i}") for i in range(2)]
    ch_mg = [chan(f"mg{i}") for i in range(2)]
    ch_vst = chan("vst")
    ch_mid = [chan(f"mst{i}") for i in range(4)]
    ch_out = [chan(f"outst{i}") for i in range(2)]

    def mm(o, l, r, st, sp, waits=(), inc=None):
        return em.emit("tensor",
                       lambda e, o=o, l=l, r=r, st=st, sp=sp:
                       e.matmul(o, l, r, start=st, stop=sp),
                       waits, s_pe if inc else None, 1)

    xq3 = xq_bf.ap().rearrange("p (cc t) -> p cc t", cc=CC)
    K3 = K_sb.ap().rearrange("p (dc t) -> p dc t", dc=DC)
    Q3 = Q_sb.ap().rearrange("p (dc t) -> p dc t", dc=DC)
    PT3 = PT.ap().rearrange("p (kt t) -> p kt t", kt=KT)
    O3 = O_sb.ap().rearrange("p (dc t) -> p dc t", dc=DC)
    y13 = y1.ap().rearrange("p (hc t) -> p hc t", hc=32)
    x13 = x1.ap().rearrange("p (cc t) -> p cc t", cc=32)
    acc3 = acc.ap().rearrange("p (dc t) -> p dc t", dc=8)
    mid4 = midbufs.ap().rearrange("p (i t) -> p i t", i=4)
    xt_own3 = xt_own.rearrange("(hc p) t -> hc p t", p=128)
    m_row = rowsA[0:1, 0:1024]
    v_row = rowsA[0:1, 1024:2048]
    r_row = rowsA[0:1, 2048:3072]
    tmp_row = rowsA[0:1, 3072:4096]

    def init_fn(e):
        e.memset(cdb[:, 0:1], 0.03125)
        e.memset(cdb[:, 1:2], -SHIFT)
        e.memset(cdb[:, 2:3], EPS)
        e.memset(cdb[:, 3:4], INV_CM)
        e.memset(cdb[:, 4:5], 1.0)
        e.memset(ones_bf[:], 1.0)
        e.memset(ones_col[:], 1.0)
        return e.memset(ones_f32[:], 1.0)
    tok_init = em.emit("vector", init_fn, (), s_dve, 1)
    tok_binit = dma("gpsimd", ch_binit, brows[0:1, 0:1024], bv_row)
    dma("sync", ch_sinit, sg_sb[:], sg_in)
    dma("sync", ch_sinit, su_sb[:], su_in)
    dma("sync", ch_sinit, sd_sb[:], sd_in)
    dma("sync", ch_sinit, bqk_sb[:], bqk_col)
    tok_sinit = dma("sync", ch_sinit, bo_sb2[:], bo_col)
    tok_bv = tok_binit
    tok_bq = tok_bk = tok_bo = tok_sg = tok_su = tok_sd = tok_sinit

    pp_gate = [tok_init] * 4

    xq_sub_gate = [None] * 8
    kslab_gate = [None, None]
    wv_gate = [None, None]
    vstage_gate = None

    def src_of(q):
        if q < 4:
            return xt_full.rearrange("(cc p) t -> p cc t", p=128)[:, :, q * T:(q + 1) * T]
        return xt_own.rearrange("(cc p) t -> p cc t", p=128)

    def load_sub(src3, sc):
        return dma("gpsimd", ch_xq[sc], xq3[:, sc * 4:(sc + 1) * 4, :],
                   src3[:, sc * 4:(sc + 1) * 4, :], [xq_sub_gate[sc]])

    def qk_section(toks_xq, slab0, bcol0, tok_b, dst3, evac0_waits=(), pre0=None):
        last = None
        for i in range(DC):
            sl = [(2 * i + h) % 2 for h in range(2)]
            if i == 0 and pre0 is not None:
                tok_sl = pre0
            else:
                tok_sl = [
                    dma("gpsimd", ch_ksl[sl[h]], kslab[:, sl[h] * 2048:(sl[h] + 1) * 2048],
                        wq_r[slab0 + i][:, h * 16:(h + 1) * 16, :].rearrange("p a b -> p (a b)"),
                        [kslab_gate[sl[h]]])
                    for h in range(2)
                ]
            ppi = i % 2
            ppt = pp[ppi]
            tok_pe = None
            for cc in range(CC):
                w = []
                if cc == 0:
                    w = [tok_sl[0], pp_gate[ppi], tok_init, tok_b]
                elif cc == 16:
                    w = [tok_sl[1]]
                if cc % 4 == 0:
                    w = w + [toks_xq[cc // 4]]
                lt = kslab[:, sl[cc // 16] * 2048 + (cc % 16) * 128:
                           sl[cc // 16] * 2048 + (cc % 16 + 1) * 128]
                for h in range(2):
                    tok = mm(ppt[:, h * 512:(h + 1) * 512], lt,
                             xq3[:, cc, h * 512:(h + 1) * 512], cc == 0, cc == CC - 1, w,
                             inc=(cc in (15, CC - 1) and h == 1))
                    w = []
                if cc == 15:
                    kslab_gate[sl[0]] = tok
                elif cc == CC - 1:
                    tok_pe = tok
            kslab_gate[sl[1]] = tok_pe
            tok_e = em.emit("scalar",
                            lambda e, d=dst3[:, i, :], s=ppt[:],
                                   b=bqk_sb[:, bcol0 + i: bcol0 + i + 1]:
                            e.activation(d, s, AF.Identity, bias=b),
                            [tok_pe] + (list(evac0_waits) if i == 0 else []), s_act, 1)
            pp_gate[ppi] = tok_e
            last = tok_pe
        return last

    src = src_of(0)
    toks_xq = [None] * 8
    toks_xq[0] = load_sub(src, 0)
    pre0 = [dma("gpsimd", ch_ksl[h], kslab[:, h * 2048:(h + 1) * 2048],
                wq_r[8][:, h * 16:(h + 1) * 16, :].rearrange("p a b -> p (a b)"))
            for h in range(2)]
    for sc in range(1, 8):
        toks_xq[sc] = load_sub(src, sc)
    for q in range(4):
        K_dst = K_sb.ap().rearrange("p (dc t) -> p dc t", dc=DC)[:, :, q * T:(q + 1) * T]
        qk_section(toks_xq, 8, 8, tok_bk, K_dst, pre0=pre0)
        next_src = src_of(q + 1)
        next_toks = [None] * 8
        next_pre0 = [None, None]
        tok_pe_v = None
        toks_ve = []
        wv4 = wvbuf.ap().rearrange("p (s j f) -> p s j f", s=2, j=4)
        for fh in range(2):
            for g in range(8):
                slot = g % 2
                tok_v = dma("gpsimd", ch_wv[slot], wv4[:, slot, :, :],
                            wv_r[4 * g:4 * g + 4, :, fh * 512:(fh + 1) * 512]
                            .rearrange("c p f -> p c f"), [wv_gate[slot]])
                w = [tok_v]
                if g == 0:
                    w = w + [tok_bv] + list(pp_gate)
                if fh == 0:
                    w = w + [toks_xq[g]]
                tok_cc = None
                for j in range(4):
                    cc = 4 * g + j
                    for tc in range(8):
                        tok_cc = mm(pp[tc // 2][:, (tc % 2) * 512:(tc % 2) * 512 + 512],
                                    xq3[:, cc, tc * 128:(tc + 1) * 128],
                                    wv4[:, slot, j, :],
                                    cc == 0, False, w, inc=(j == 3 and tc == 7))
                        w = []
                wv_gate[slot] = tok_cc
                if fh == 1:
                    xq_sub_gate[g] = tok_cc
                    if g >= 1:
                        next_toks[g - 1] = load_sub(next_src, g - 1)
                    if g in (2, 3):
                        nslab = 8 if q < 3 else 0
                        next_pre0[g - 2] = dma(
                            "gpsimd", ch_ksl[g - 2],
                            kslab[:, (g - 2) * 2048:(g - 1) * 2048],
                            wq_r[nslab][:, (g - 2) * 16:(g - 1) * 16, :].rearrange("p a b -> p (a b)"),
                            [kslab_gate[g - 2]])
            for tc in range(8):
                tok_pe_v = mm(pp[tc // 2][:, (tc % 2) * 512:(tc % 2) * 512 + 512],
                              ones_bf[0:1, 0:128],
                              brows[0:1, fh * 512:(fh + 1) * 512],
                              False, True, (), inc=(tc == 7))
            toks_ve = []
            for tc in range(8):
                toks_ve.append(em.emit(
                    "scalar",
                    lambda e, d=V_stage[:, tc * 1024 + fh * 512: tc * 1024 + fh * 512 + 512],
                           s=pp[tc // 2][:, (tc % 2) * 512:(tc % 2) * 512 + 512]:
                    e.copy(d, s),
                    [tok_pe_v] + ([vstage_gate] if (fh == 0 and tc == 0) else []),
                    s_act, 1))
            for i in range(4):
                pp_gate[i] = toks_ve[2 * i + 1]
        tok_vst = dma("sync", ch_vst, V_d[q * 8:(q + 1) * 8].rearrange("tc p f -> p tc f"),
                      V_stage.ap().rearrange("p (tc f) -> p tc f", tc=8),
                      [toks_ve[-1]])
        next_toks[7] = load_sub(next_src, 7)
        toks_xq = next_toks
        pre0 = next_pre0 if next_pre0[0] is not None else None
        vstage_gate = tok_vst
    tok_q_last = qk_section(toks_xq, 0, 0, tok_bq, Q3, evac0_waits=[vstage_gate], pre0=pre0)
    a_done_act = (s_act, s_act.n)

    pt_done = tok_rec = pv_done_dve = pv_pe_done = None
    if lvl >= 1:
        exp_toks = []
        for kt in range(KT):
            ppi = kt % 2
            ppt = pp[ppi]
            w = [pp_gate[ppi]] + ([a_done_act] if kt == 0 else [])
            tok_s = None
            for dc in range(DC):
                for qh in range(2):
                    tok_s = mm(ppt[:, qh * 512:(qh + 1) * 512],
                               K3[:, dc, kt * 128:(kt + 1) * 128],
                               Q3[:, dc, qh * 512:(qh + 1) * 512],
                               dc == 0, dc == DC - 1, w,
                               inc=(dc == DC - 1 and qh == 1))
                    w = []
            tok_e = em.emit("scalar",
                            lambda e, d=PT3[:, kt, :], s=ppt[:]:
                            e.activation(d, s, AF.Exp, bias=-SHIFT, scale=0.03125),
                            [tok_s], s_act, 1)
            pp_gate[ppi] = tok_e
            exp_toks.append(tok_e)
            if kt >= 1:
                for qh in range(2):
                    w = [exp_toks[kt - 1]] + ([pp_gate[2]] if kt == 1 else [])
                    mm(pp[2][0:1, qh * 512:(qh + 1) * 512], ones_col[:, 0:1],
                       PT3[:, kt - 1, qh * 512:(qh + 1) * 512],
                       kt == 1, False, w)
        pt_done = (s_act, s_act.n)
        for qh in range(2):
            tok_sums = mm(pp[2][0:1, qh * 512:(qh + 1) * 512], ones_col[:, 0:1],
                          PT3[:, KT - 1, qh * 512:(qh + 1) * 512],
                          False, True, [exp_toks[KT - 1]], inc=(qh == 1))
        def rec_fn(e):
            e.activation(rowsA[0:1, 0:512], pp[2][0:1, 0:512], AF.Ln)
            e.activation(rowsA[0:1, 512:1024], pp[2][0:1, 512:1024], AF.Ln)
            e.activation(rowsB[0:1, 0:512], rowsA[0:1, 0:512], AF.Exp, scale=-1.0)
            return e.activation(rowsB[0:1, 512:1024], rowsA[0:1, 512:1024], AF.Exp, scale=-1.0)
        tok_rec = em.emit("scalar", rec_fn, [tok_sums], s_act, 1)
        pp_gate[2] = tok_rec
        tok_bc = None
        for qh in range(2):
            tok_bc = mm(pp[3][:, qh * 512:(qh + 1) * 512], ones_f32[0:1, :],
                        rowsB[0:1, qh * 512:(qh + 1) * 512], True, True,
                        [tok_rec, pp_gate[3]] if qh == 0 else [], inc=(qh == 1))
        tok_rb = em.emit("scalar", lambda e: e.copy(rb_sb[:], pp[3][:]),
                         [tok_bc], s_act, 1)
        pp_gate[3] = tok_rb
        vb_gate = [tok_q_last] * 2
        vb4 = vbuf.ap().rearrange("p (s k f) -> p s k f", s=2, k=2)
        for qh in range(2):
            tok_pe_pv = None
            for kp in range(KT // 2):
                slot = kp % 2
                tok_v = dma("sync", ch_vb[slot], vb4[:, slot, :, :],
                            V_d[2 * kp:2 * kp + 2].rearrange("k p f -> p k f"),
                            [vb_gate[slot]] + ([vstage_gate] if kp == 0 else []))
                w = [tok_v] + (list(pp_gate) if kp == 0 else [])
                for ki in range(2):
                    kc = 2 * kp + ki
                    for dc in range(DC):
                        tok_pe_pv = mm(pp[dc // 2][:, (dc % 2) * 512:(dc % 2) * 512 + 512],
                                       vb4[:, slot, ki, dc * 128:(dc + 1) * 128],
                                       PT3[:, kc, qh * 512:(qh + 1) * 512],
                                       kc == 0, kc == KT - 1, w,
                                       inc=(ki == 1 and dc == DC - 1))
                        w = []
                vb_gate[slot] = tok_pe_pv
            toks = []
            for dc in range(8):
                toks.append(em.emit(
                    "vector",
                    lambda e, d=O3[:, dc, qh * 512:(qh + 1) * 512],
                           s=pp[dc // 2][:, (dc % 2) * 512:(dc % 2) * 512 + 512],
                           rb=rb_sb[:, qh * 512:(qh + 1) * 512]:
                    e.scalar_tensor_tensor(d, s, 1.0, rb, ALU.mult, ALU.mult),
                    [tok_pe_pv, tok_rb], s_dve, 1))
            for i in range(4):
                pp_gate[i] = toks[2 * i + 1]
        pv_done_dve = (s_dve, s_dve.n)
        pv_pe_done = (s_pe, s_pe.n)

    def ln_rows(tok_in, mb_dst, rb_dst, gates):
        def rows_fn(e):
            e.tensor_scalar_mul(m_row[0:1, 0:512], pp[2][0:1, 0:512], INV_CM)
            e.tensor_scalar_mul(m_row[0:1, 512:1024], pp[2][0:1, 512:1024], INV_CM)
            e.tensor_scalar_mul(v_row[0:1, 0:512], pp[3][0:1, 0:512], INV_CM)
            e.tensor_scalar_mul(v_row[0:1, 512:1024], pp[3][0:1, 512:1024], INV_CM)
            e.tensor_tensor(tmp_row, m_row, m_row, ALU.mult)
            return e.tensor_tensor(v_row, v_row, tmp_row, ALU.subtract)
        tok_rows = em.emit("vector", rows_fn, [tok_in], s_dve, 1)
        def rstd_fn(e):
            e.activation(tmp_row, v_row, AF.Ln, bias=EPS)
            return e.activation(r_row, tmp_row, AF.Exp, scale=-0.5)
        tok_r = em.emit("scalar", rstd_fn, [tok_rows], s_act, 1)
        tok_bc = None
        for i, src_row in enumerate((m_row, r_row)):
            for qh in range(2):
                tok_bc = mm(pp[i][:, qh * 512:(qh + 1) * 512], ones_f32[0:1, :],
                            src_row[0:1, qh * 512:(qh + 1) * 512], True, True,
                            ([tok_r] + gates) if (i == 0 and qh == 0) else [],
                            inc=(i == 1 and qh == 1))
        def cp_fn(e):
            e.copy(mb_dst[:], pp[0][:])
            return e.copy(rb_dst[:], pp[1][:])
        tok_mb = em.emit("scalar", cp_fn, [tok_bc], s_act, 1)
        return tok_mb

    tok_stats = tok_mb = x1_done = None
    if lvl >= 2:
        wo_gate = [pv_done_dve, pv_done_dve]
        xc_gate = [pv_done_dve, pv_done_dve]
        sq_gate = [None, None]
        for hc in range(32):
            slot = hc % 2
            tok_w = dma("gpsimd", ch_wo[slot], wobuf[:, slot * DH:(slot + 1) * DH],
                        wo_r[hc].rearrange("p a b -> p (a b)"), [wo_gate[slot]])
            tok_x = dma("sync", ch_xc[slot], xcbuf[:, slot * T:(slot + 1) * T],
                        xt_own3[hc], [xc_gate[slot]])
            ppi = hc % 2
            ppt = pp[ppi]
            w = [tok_w, pp_gate[ppi], tok_bo] + ([pv_done_dve] if hc == 0 else [])
            tok_y = None
            for dc in range(DC):
                for qh in range(2):
                    tok_y = mm(ppt[:, qh * 512:(qh + 1) * 512],
                               wobuf[:, slot * DH + dc * 128: slot * DH + (dc + 1) * 128],
                               O3[:, dc, qh * 512:(qh + 1) * 512], dc == 0, dc == DC - 1, w,
                               inc=(dc == DC - 1 and qh == 1))
                    w = []
            wo_gate[slot] = tok_y
            tok_add = em.emit("vector",
                              lambda e, d=y13[:, hc, :], s=ppt[:],
                                     x=xcbuf[:, slot * T:(slot + 1) * T],
                                     b=bo_sb2[:, hc:hc + 1]:
                              e.scalar_tensor_tensor(d, s, b, x, ALU.add, ALU.add),
                              [tok_y, tok_x], s_dve, 1)
            xc_gate[slot] = tok_add
            pp_gate[ppi] = tok_add
            tok_sc = em.emit("vector",
                             lambda e, hc=hc, slot=slot:
                             e.tensor_mul(sqbuf[:, slot * T:(slot + 1) * T],
                                          y13[:, hc, :], y13[:, hc, :]),
                             [tok_add, sq_gate[slot]], s_dve, 1)
            w = [tok_add, tok_sc] + ([pp_gate[2], pp_gate[3]] if hc == 0 else [])
            for qh in range(2):
                mm(pp[2][0:1, qh * 512:(qh + 1) * 512], ones_col[:, 0:1],
                   y13[:, hc, qh * 512:(qh + 1) * 512],
                   hc == 0, hc == 31, w)
                w = []
                tok_stats = mm(pp[3][0:1, qh * 512:(qh + 1) * 512], ones_col[:, 0:1],
                               sqbuf[:, slot * T + qh * 512: slot * T + qh * 512 + 512],
                               hc == 0, hc == 31, (), inc=(qh == 1))
            sq_gate[slot] = tok_stats

        tok_mb = ln_rows(tok_stats, mb_sb, rsb_sb, [pp_gate[0], pp_gate[1]])
        for i in range(4):
            pp_gate[i] = tok_mb
        x1_toks = []
        for hc in range(32):
            def ap_fn(e, hc=hc):
                e.tensor_tensor(t0buf[:], y13[:, hc, :], mb_sb[:], ALU.subtract)
                return e.tensor_tensor(x13[:, hc, :], t0buf[:], rsb_sb[:], ALU.mult)
            x1_toks.append(em.emit("vector", ap_fn, [tok_mb] if hc == 0 else [], s_dve, 1))
        x1_done = x1_toks[-1]

    mid_store = [None] * HCN
    d1_last_mid = None
    if lvl >= 3:
        wg_gate = [x1_toks[3], x1_toks[7]]
        wu_gate = [x1_toks[11], x1_toks[15]]
        gt_gate = [None, None]
        mid_gate = [None, None, None, None]
        for hc in range(HCN):
            slot = hc % 2
            tok_g = dma("gpsimd", ch_wg[slot], wgbuf[:, slot * 4096:(slot + 1) * 4096],
                        wg_r[hc].rearrange("p a b -> p (a b)"), [wg_gate[slot]])
            tok_u = dma("gpsimd", ch_wu[slot], wubuf[:, slot * 4096:(slot + 1) * 4096],
                        wu_r[hc].rearrange("p a b -> p (a b)"), [wu_gate[slot]])
            ppg = pp[hc % 2]
            ppu = pp[2 + hc % 2]
            w = [tok_g, pp_gate[hc % 2]] + ([x1_done] if hc == 0 else [])
            tok_gmm = None
            for cc in range(CC):
                for qh in range(2):
                    tok_gmm = mm(ppg[:, qh * 512:(qh + 1) * 512],
                                 wgbuf[:, slot * 4096 + cc * 128: slot * 4096 + (cc + 1) * 128],
                                 x13[:, cc, qh * 512:(qh + 1) * 512],
                                 cc == 0, cc == CC - 1, w,
                                 inc=(cc == CC - 1 and qh == 1))
                    w = []
            w = [tok_u, pp_gate[2 + hc % 2]]
            tok_umm = None
            for cc in range(CC):
                for qh in range(2):
                    tok_umm = mm(ppu[:, qh * 512:(qh + 1) * 512],
                                 wubuf[:, slot * 4096 + cc * 128: slot * 4096 + (cc + 1) * 128],
                                 x13[:, cc, qh * 512:(qh + 1) * 512],
                                 cc == 0, cc == CC - 1, w,
                                 inc=(cc == CC - 1 and qh == 1))
                    w = []
            wg_gate[slot] = tok_gmm
            wu_gate[slot] = tok_umm
            def ev_fn(e, slot=slot, ppg=ppg, ppu=ppu):
                e.activation(gtbuf[:, slot * T:(slot + 1) * T], ppg[:], AF.Silu,
                             scale=sg_sb[:, 0:1])
                return e.activation(utbuf[:, slot * T:(slot + 1) * T], ppu[:],
                                    AF.Identity, scale=su_sb[:, 0:1])
            tok_si = em.emit("scalar", ev_fn,
                             [tok_gmm, tok_umm, gt_gate[slot], tok_sg, tok_su],
                             s_act, 1)
            pp_gate[hc % 2] = tok_si
            pp_gate[2 + hc % 2] = tok_si
            ms = hc % 4
            tok_mid = em.emit("vector",
                              lambda e, d=mid4[:, ms, :],
                                     a=gtbuf[:, slot * T:(slot + 1) * T],
                                     b=utbuf[:, slot * T:(slot + 1) * T]:
                              e.tensor_tensor(d, a, b, ALU.mult),
                              [tok_si, mid_gate[ms]], s_dve, 1)
            gt_gate[slot] = tok_mid
            d1_last_mid = tok_mid
            tok_mst = dma("sync", ch_mid[ms], mid_d[hc], mid4[:, ms, :], [tok_mid])
            mid_gate[ms] = tok_mst
            mid_store[hc] = tok_mst

    if lvl >= 4:
        GDEF = 6
        gstarts = list(range(0, 84, GDEF)) + [84]
        gsizes = [GDEF] * 14 + [2]
        wdg3 = wdgbuf.ap().rearrange("p (s g x) -> p s g x", s=2, g=GDEF)
        midg3 = midgbuf.ap().rearrange("p (s g t) -> p s g t", s=2, g=GDEF)
        wdg_gate = [x1_toks[21], x1_toks[27]]
        midg_gate = [None, None]
        y2b_gate = [None, None]
        tok_stats2 = None
        for qtr in range(4):
            acc_tok = [None] * 8
            for gi, (g0, G) in enumerate(zip(gstarts, gsizes)):
                slot = gi % 2
                tok_wd = dma("gpsimd", ch_wdg[slot], wdg3[:, slot, 0:G, :],
                             wd_r[qtr, g0:g0 + G].rearrange("g p a b -> p g (a b)"),
                             [wdg_gate[slot]])
                tok_md = dma("scalar", ch_mg[slot], midg3[:, slot, 0:G, :],
                             mid_d[g0:g0 + G].rearrange("g p t -> p g t"),
                             [midg_gate[slot]] + [mid_store[h] for h in range(g0, g0 + G)])
                last_pe = None
                for dc in range(8):
                    ppi = dc % 2
                    ppt = pp[ppi]
                    w = [pp_gate[ppi]] + ([tok_wd, tok_md] if dc < 2 else [])
                    tok_d = None
                    for i in range(G):
                        for qh in range(2):
                            tok_d = mm(ppt[:, qh * 512:(qh + 1) * 512],
                                       wdg3[:, slot, i, dc * 128:(dc + 1) * 128],
                                       midg3[:, slot, i, qh * 512:(qh + 1) * 512],
                                       i == 0, i == G - 1, w,
                                       inc=(i == G - 1 and qh == 1))
                            w = []
                    last_pe = tok_d
                    if gi == 0:
                        tok_a = em.emit("vector",
                                        lambda e, d=acc3[:, dc, :], s=ppt[:]:
                                        e.tensor_copy(d, s), [tok_d], s_dve, 1)
                    else:
                        tok_a = em.emit("vector",
                                        lambda e, d=acc3[:, dc, :], s=ppt[:]:
                                        e.tensor_tensor(d, s, d, ALU.add),
                                        [tok_d], s_dve, 1)
                    pp_gate[ppi] = tok_a
                    acc_tok[dc] = tok_a
                wdg_gate[slot] = last_pe
                midg_gate[slot] = last_pe
            for dc in range(8):
                ch = qtr * 8 + dc
                slot = dc % 2
                tok_y2 = em.emit("vector",
                                 lambda e, d=x13[:, ch, :],
                                        a=acc3[:, dc, :], x=x13[:, ch, :]:
                                 e.scalar_tensor_tensor(d, a, sd_sb[:, 0:1], x,
                                                        ALU.mult, ALU.add),
                                 [acc_tok[dc], tok_sd], s_dve, 1)
                tok_sc2 = em.emit("vector",
                                  lambda e, slot=slot, ch=ch:
                                  e.tensor_mul(sqbuf[:, slot * T:(slot + 1) * T],
                                               x13[:, ch, :], x13[:, ch, :]),
                                  [tok_y2, y2b_gate[slot]] +
                                  ([tok_stats, sq_gate[0], sq_gate[1]] if ch == 0 else []),
                                  s_dve, 1)
                w = [tok_y2, tok_sc2] + ([pp_gate[2], pp_gate[3]] if ch == 0 else [])
                for qh in range(2):
                    mm(pp[2][0:1, qh * 512:(qh + 1) * 512], ones_col[:, 0:1],
                       x13[:, ch, qh * 512:(qh + 1) * 512],
                       ch == 0, ch == 31, w)
                    w = []
                    tok_stats2 = mm(pp[3][0:1, qh * 512:(qh + 1) * 512], ones_col[:, 0:1],
                                    sqbuf[:, slot * T + qh * 512: slot * T + qh * 512 + 512],
                                    ch == 0, ch == 31, (), inc=(qh == 1))
                y2b_gate[slot] = tok_stats2

        tok_m2b = ln_rows(tok_stats2, m2b_sb, r2b_sb, [pp_gate[0], pp_gate[1]])
        ob_gate = [None, None, None, None]
        ch_out = ch_out + [chan("outst2"), chan("outst3")]
        out3 = out_t.rearrange("(hc p) t -> hc p t", p=128)
        for ch in range(32):
            slot = ch % 4
            eng = "gpsimd" if ch % 6 == 5 else "vector"
            t02 = t02gbuf if eng == "gpsimd" else t02buf
            def ap2_fn(e, slot=slot, ch=ch, t02=t02):
                e.tensor_tensor(t02[:], x13[:, ch, :], m2b_sb[:], ALU.subtract)
                return e.tensor_tensor(y2buf[:, slot * T:(slot + 1) * T], t02[:],
                                       r2b_sb[:], ALU.mult)
            tok_o = em.emit(eng, ap2_fn, [ob_gate[slot], tok_m2b],
                            s_gp if eng == "gpsimd" else s_dve, 1)
            tok_out = dma("sync", ch_out[slot], out3[ch],
                          y2buf[:, slot * T:(slot + 1) * T], [tok_o])
            ob_gate[slot] = tok_out

    for name in dumps:
        if name == "qkv":
            dq = nc.dram_tensor("d_q", [128, DC * T], BF16, kind="ExternalOutput").ap()
            dk = nc.dram_tensor("d_k", [128, DC * S], BF16, kind="ExternalOutput").ap()
            dv = nc.dram_tensor("d_v", [KT, 128, DH], BF16, kind="ExternalOutput").ap()
            dma("sync", ch_out[0], dq, Q_sb.ap(), [a_done_act])
            dma("sync", ch_out[0], dk, K_sb.ap(), [a_done_act])
            dma("sync", ch_out[0], dv, V_d[:], [vstage_gate])
        if name == "pt" and lvl >= 1:
            dp = nc.dram_tensor("d_pt", [128, KT * T], BF16, kind="ExternalOutput").ap()
            dr = nc.dram_tensor("d_rec", [1, 1024], F32, kind="ExternalOutput").ap()
            dma("sync", ch_out[0], dp, PT.ap(), [pt_done, pv_pe_done])
            dma("sync", ch_out[0], dr, rowsB[0:1, :], [tok_rec])
        if name == "o" and lvl >= 1:
            do = nc.dram_tensor("d_o", [128, DC * T], BF16, kind="ExternalOutput").ap()
            dma("sync", ch_out[0], do, O_sb.ap(), [pv_done_dve])
        if name == "x1" and lvl >= 2:
            dx = nc.dram_tensor("d_x1", [128, 32 * T], BF16, kind="ExternalOutput").ap()
            dma("sync", ch_out[0], dx, x1.ap(), [(s_dve, s_dve.n)])
        if name == "mid" and lvl >= 3:
            dm = nc.dram_tensor("d_mid", [HCN, 128, T], BF16, kind="ExternalOutput").ap()
            dma("sync", ch_out[0], dm, mid_d[:], [(c, c.n) for c in ch_mid])
        if name == "y2" and lvl >= 4:
            da = nc.dram_tensor("d_y2", [128, 32 * T], BF16, kind="ExternalOutput").ap()
            dma("sync", ch_out[0], da, x1.ap(), [tok_stats2])

    final_waits = [(c, c.n) for c in [ch_vst] + ch_mid + ch_out]
    em.emit("sync", lambda e: None, final_waits)

    with nc.Block() as block:
        em.build(block)
    return nc


def _prep(inputs):
    x = np.asarray(inputs["L0_attn_x"], dtype=np.float32)
    Wqkv = np.asarray(inputs["L0_attn_Wqkv"], dtype=np.float32)
    bqkv = np.asarray(inputs["L0_attn_bqkv"], dtype=np.float32)
    Wo = np.asarray(inputs["L0_attn_Wo"], dtype=np.float32)
    bo = np.asarray(inputs["L0_attn_bo"], dtype=np.float32)
    wg = np.asarray(inputs["L0_ffn_w_gate"], dtype=np.int32)
    wu = np.asarray(inputs["L0_ffn_w_up"], dtype=np.int32)
    wd = np.asarray(inputs["L0_ffn_w_down"], dtype=np.int32)

    xt = [np.ascontiguousarray(x[b].T) for b in range(B)]
    com = {
        "wq_r": np.ascontiguousarray(
            Wqkv[:, 0:2048].reshape(CC, 128, 16, 128).transpose(2, 1, 0, 3)),
        "wv_r": np.ascontiguousarray(Wqkv[:, 2048:3072].reshape(CC, 128, DH)),
        "bqk_col": np.ascontiguousarray(bqkv[0:2048].reshape(16, 128).T),
        "bv_row": np.ascontiguousarray(bqkv[2048:3072])[None, :],
        "wo_r": np.ascontiguousarray(Wo.reshape(DC, 128, 32, 128).transpose(2, 1, 0, 3)),
        "bo_col": np.ascontiguousarray(bo.reshape(32, 128).T),
        "wg_r": np.ascontiguousarray(wg.reshape(HCN, 128, CC, 128).transpose(0, 3, 2, 1)),
        "wu_r": np.ascontiguousarray(wu.reshape(HCN, 128, CC, 128).transpose(0, 3, 2, 1)),
        "wd_r": np.ascontiguousarray(
            wd.reshape(4, 8, 128, HCN, 128).transpose(0, 3, 4, 1, 2)),
        "sg_in": np.ascontiguousarray(np.broadcast_to(
            np.float32(inputs["L0_ffn_scale_gate"]).reshape(1, 1), (128, 1))),
        "su_in": np.ascontiguousarray(np.broadcast_to(
            np.float32(inputs["L0_ffn_scale_up"]).reshape(1, 1), (128, 1))),
        "sd_in": np.ascontiguousarray(np.broadcast_to(
            np.float32(inputs["L0_ffn_scale_down"]).reshape(1, 1), (128, 1))),
    }
    in_maps = []
    for c in range(8):
        b, q = c // 4, c % 4
        m = dict(com)
        m["xt_full"] = xt[b]
        m["xt_own"] = np.ascontiguousarray(xt[b][:, q * T:(q + 1) * T])
        in_maps.append(m)
    return in_maps


def _run(inputs, dumps=(), upto="FIN"):
    in_maps = _prep(inputs)
    nc = build(dumps=dumps, upto=upto)
    return run_bass_kernel_spmd(nc, in_maps, core_ids=list(range(8)))


def kernel(**inputs):
    res = _run(inputs, dumps=_DEBUG_DUMPS)
    out = np.empty((B, S, CM), dtype=np.float32)
    for c in range(8):
        b, q = c // 4, c % 4
        out[b, q * T:(q + 1) * T, :] = res.results[c]["out"].T
    if _DEBUG_DUMPS:
        kernel._last = res
    return out
